# version 10
# baseline (speedup 1.0000x reference)
import math
import numpy as np
from contextlib import ExitStack

import concourse.bass as bass
import concourse.mybir as mybir
from concourse.alu_op_type import AluOpType
from concourse.bass_utils import run_bass_kernel_spmd

F32 = mybir.dt.float32
BF16 = mybir.dt.bfloat16
AF = mybir.ActivationFunctionType
AX = mybir.AxisListType
OP = AluOpType

N_CORES = 8


class Sched:
    ENGINES = ["tensor", "vector", "scalar", "gpsimd", "sync"]

    def __init__(self):
        self.steps = []
        self.counts = {}

    def step(self, engine, emitter, wait=(), inc=None, amt=1, tag=""):
        tok = None
        if inc is not None:
            self.counts[inc] = self.counts.get(inc, 0) + amt
            tok = (inc, self.counts[inc])
        wait = [w for w in wait if w is not None]
        self.steps.append((engine, emitter, tuple(wait), inc, amt, tag))
        return tok

    def barrier(self):
        return [(k, v) for k, v in self.counts.items() if v > 0]

    def dma(self, queue, out, in_, wait=(), inc=None, tag=""):
        return self.step(queue, lambda e, o=out, i=in_: e.dma_start(out=o, in_=i),
                         wait=wait, inc=inc, amt=16, tag=tag)

    def emit(self, block, sems):
        for eng_name in self.ENGINES:
            mysteps = [st for st in self.steps if st[0] == eng_name]
            if not mysteps:
                continue

            def make(fn_steps):
                def _(eng):
                    seen = {}
                    for (_, emitter, waits, inc, amt, tag) in fn_steps:
                        for (sem, val) in waits:
                            if val <= 0 or seen.get(sem, 0) >= val:
                                continue
                            eng.wait_ge(sems[sem], val)
                            seen[sem] = val
                        instr = emitter(eng)
                        if inc is not None:
                            if instr is None:
                                eng.sem_inc(sems[inc], amt)
                            else:
                                instr.then_inc(sems[inc], amt)
                return _

            getattr(block, eng_name)(make(mysteps))


def lt(*toks):
    best = {}
    for t in toks:
        if t is None:
            continue
        if isinstance(t, (list, tuple)) and t and isinstance(t[0], (list, tuple)):
            for u in t:
                if u is not None:
                    best[u[0]] = max(best.get(u[0], 0), u[1])
            continue
        if isinstance(t, list):
            for u in t:
                if u is not None:
                    best[u[0]] = max(best.get(u[0], 0), u[1])
            continue
        best[t[0]] = max(best.get(t[0], 0), t[1])
    return [(s, v) for s, v in best.items()]


def make_cfg(full=True):
    if full:
        c = dict(B=2, T=2048, C=1024, NH=16, D=32, DV=64, E=64, EG=8, I=512, ISH=1024)
    else:
        import os as _os
        _rt = int(_os.environ.get('RED_T', '512'))
        _ri = int(_os.environ.get('RED_I', '128'))
        c = dict(B=1, T=_rt, C=1024, NH=16, D=32, DV=64, E=16, EG=8, I=_ri, ISH=2*_ri)
    c["NH2"] = 2 * c["NH"]
    c["CH"] = c["C"] // 128
    c["NTT"] = c["T"] // 128
    c["TG"] = min(512, c["T"])
    c["NTG"] = c["T"] // c["TG"]
    c["QPG"] = c["TG"] // 128
    c["IC"] = c["I"] // 128
    c["ICS"] = c["ISH"] // 128
    c["NSQ"] = max(1, c["NH2"] // 16)
    c["HPSQ"] = c["NH"] // c["NSQ"]
    c["EPG"] = c["E"] // c["EG"]
    c["LAM_INIT"] = 0.2
    c["EPS"] = 1e-5
    c["SCALE_MOE"] = 2.5
    c["ROPE_BASE"] = 10000.0
    c["SM_SCALE"] = 1.0 / math.sqrt(c["D"])
    return c


def host_constants(cfg):
    T, C, D, NH2, E = cfg["T"], cfg["C"], cfg["D"], cfg["NH2"], cfg["E"]
    half = D // 2
    con = {}
    con["identf"] = np.eye(128, dtype=np.float32)
    con["identb"] = np.eye(128, dtype=np.float32)
    theta = cfg["ROPE_BASE"] ** (-(np.arange(half, dtype=np.float64) * 2.0 / D))
    ang = np.arange(T, dtype=np.float64)[None, :] * theta[:, None]
    rows = np.arange(128) % half
    con["cosr"] = np.cos(ang)[rows, :].astype(np.float32)
    con["sinr"] = np.sin(ang)[rows, :].astype(np.float32)
    nch = NH2 * D // 128
    P1 = np.zeros((nch, 128, 128), dtype=np.float32)
    P2 = np.zeros((nch, 128, 128), dtype=np.float32)
    for h in range(NH2):
        dst_c, dst_r0 = h // 4, 32 * (h % 4)
        src_r0 = 16 * (h % 8)
        for j in range(half):
            P1[dst_c, src_r0 + j, dst_r0 + j] = 1.0
            P2[dst_c, src_r0 + j, dst_r0 + 16 + j] = 1.0
    con["permx1"] = np.ascontiguousarray(P1.transpose(1, 0, 2).reshape(128, nch * 128))
    con["permx2"] = np.ascontiguousarray(P2.transpose(1, 0, 2).reshape(128, nch * 128))
    QPG, TG = cfg["QPG"], cfg["TG"]
    cm = np.zeros((QPG, 128, TG), dtype=np.float32)
    for v in range(QPG):
        q = v * 128 + np.arange(128)
        k = np.arange(TG)
        cm[v] = (k[None, :] <= q[:, None]).astype(np.float32)
    con["cmask"] = np.ascontiguousarray(cm.transpose(1, 0, 2).reshape(128, QPG * TG))
    con["neg_iota"] = np.tile(-np.arange(E, dtype=np.float32) * 1e-20, (128, 1))
    con["ones_ln"] = np.full((128, 32), 1.0 / C, dtype=np.float32)
    con["ones_b32"] = np.full((32, 128), 1.0 / 32, dtype=np.float32)
    sel = np.zeros((64, 32 * 128), dtype=np.float32)
    for j in range(32):
        sel[j, 128 * j:128 * j + 128] = 1.0
        sel[32 + j, 128 * j:128 * j + 128] = 1.0
    con["sel_bank"] = sel
    return con


def permute_wattn(cfg, W):
    C, NH2, D = cfg["C"], cfg["NH2"], cfg["D"]
    half = D // 2
    perm = np.zeros(C, dtype=np.int64)
    for h in range(NH2):
        for j in range(half):
            perm[128 * (h // 8) + 16 * (h % 8) + j] = h * D + 2 * j
            perm[C // 2 + 128 * (h // 8) + 16 * (h % 8) + j] = h * D + 2 * j + 1
    Wq = np.asarray(W)[:, 0:C][:, perm]
    Wk = np.asarray(W)[:, C:2 * C][:, perm]
    Wv = np.asarray(W)[:, 2 * C:3 * C]
    return np.ascontiguousarray(np.concatenate([Wq, Wk, Wv], axis=1))


def build_program(cfg):
    nc = bass.Bass("TRN2", target_bir_lowering=False)
    B, T, C, CH = cfg["B"], cfg["T"], cfg["C"], cfg["CH"]
    NH, NH2, D, DV = cfg["NH"], cfg["NH2"], cfg["D"], cfg["DV"]
    E, I, ISH, EG, EPG = cfg["E"], cfg["I"], cfg["ISH"], cfg["EG"], cfg["EPG"]
    IC, ICS = cfg["IC"], cfg["ICS"]
    TG, NTG, NTT, QPG = cfg["TG"], cfg["NTG"], cfg["NTT"], cfg["QPG"]
    NSQ, HPSQ = cfg["NSQ"], cfg["HPSQ"]
    SMS = cfg["SM_SCALE"]
    EPS = cfg["EPS"]
    nch = NH2 * D // 128
    ICH = ICS // 2

    dram = {}
    dram["x"] = nc.dram_tensor("x", [B, T, C], F32, kind="ExternalInput").ap()
    dram["wattn"] = nc.dram_tensor("wattn", [C, 2 * nch * 128 + NH * DV], F32, kind="ExternalInput").ap()
    dram["wproj"] = nc.dram_tensor("wproj", [C, C], F32, kind="ExternalInput").ap()
    dram["router_w"] = nc.dram_tensor("router_w", [E, C], F32, kind="ExternalInput").ap()
    dram["Wg"] = nc.dram_tensor("Wg", [E, C, I], F32, kind="ExternalInput").ap()
    dram["Wu"] = nc.dram_tensor("Wu", [E, C, I], F32, kind="ExternalInput").ap()
    dram["Wd"] = nc.dram_tensor("Wd", [E, I, C], F32, kind="ExternalInput").ap()
    dram["SWg"] = nc.dram_tensor("SWg", [C, ISH], F32, kind="ExternalInput").ap()
    dram["SWu"] = nc.dram_tensor("SWu", [C, ISH], F32, kind="ExternalInput").ap()
    dram["SWd"] = nc.dram_tensor("SWd", [ISH, C], F32, kind="ExternalInput").ap()
    dram["lam_rep"] = nc.dram_tensor("lam_rep", [128, 4 * D], F32, kind="ExternalInput").ap()
    dram["ebias_rep"] = nc.dram_tensor("ebias_rep", [128, E], F32, kind="ExternalInput").ap()
    for name, shape in [("identf", [128, 128]), ("identb", [128, 128]),
                        ("cosr", [128, T]), ("sinr", [128, T]),
                        ("permx1", [128, nch * 128]), ("permx2", [128, nch * 128]),
                        ("cmask", [128, QPG * TG]), ("neg_iota", [128, E]),
                        ("ones_ln", [128, 32]), ("ones_b32", [32, 128]),
                        ("sel_bank", [64, 32 * 128])]:
        dram[name] = nc.dram_tensor("c_" + name, shape, F32, kind="ExternalInput").ap()
    xt_scr = nc.dram_tensor("xt_scratch", [B, C, T], F32, kind="Internal").ap()
    dT_scr = nc.dram_tensor("dT_scratch", [B, C, T], BF16, kind="Internal").ap()
    y_out = nc.dram_tensor("y", [B, T, C], F32, kind="ExternalOutput").ap()

    S = Sched()
    sems = {}
    SEM_NAMES = ["t", "v", "s", "p", "dc", "dcg", "dx", "dx0", "dx1", "dsc0", "dsc1",
                 "dxs", "dwa", "dw0", "dw1", "o0", "o1", "dd0", "dd1", "ddl",
                 "dwg0", "dwg1", "dwu0", "dwu1", "dwd0", "dwd1"]

    with ExitStack() as top:
        identf = top.enter_context(nc.sbuf_tensor("identf", [128, 128], F32))
        identb = top.enter_context(nc.sbuf_tensor("identb", [128, 128], BF16))
        nib = top.enter_context(nc.sbuf_tensor("nib", [128, E], F32))
        olnb = top.enter_context(nc.sbuf_tensor("olnb", [128, 32], BF16))
        ob32 = top.enter_context(nc.sbuf_tensor("ob32", [32, 128], BF16))
        lamr = top.enter_context(nc.sbuf_tensor("lamr", [128, 4 * D], F32))
        ebr = top.enter_context(nc.sbuf_tensor("ebr", [128, E], F32))
        lamv = top.enter_context(nc.sbuf_tensor("lamv", [128, 8], F32))
        epsc = top.enter_context(nc.sbuf_tensor("epsc", [128, 1], F32))

        for s_ in SEM_NAMES:
            sems[s_] = top.enter_context(nc.semaphore(s_))

        S.dma("sync", identf[:], dram["identf"], inc="dc")
        S.dma("gpsimd", identb[:], dram["identb"], inc="dcg")
        S.dma("sync", nib[:], dram["neg_iota"], inc="dc")
        S.dma("gpsimd", olnb[:], dram["ones_ln"], inc="dcg")
        S.dma("gpsimd", ob32[:], dram["ones_b32"], inc="dcg")
        S.dma("sync", lamr[:], dram["lam_rep"], inc="dc")
        S.dma("sync", ebr[:], dram["ebias_rep"], inc="dc")
        CONST = [("dc", S.counts["dc"]), ("dcg", S.counts["dcg"])]

        S.step("vector", lambda e: e.memset(epsc[:], 1e-5), inc="v")
        t1_ = S.step("vector", lambda e: (
            e.tensor_tensor(lamr[:, 0:D], lamr[:, 0:D], lamr[:, D:2 * D], op=OP.mult),
            e.tensor_tensor(lamr[:, 2 * D:3 * D], lamr[:, 2 * D:3 * D], lamr[:, 3 * D:4 * D], op=OP.mult))[-1],
            wait=CONST, inc="v")
        t2_ = S.step("vector", lambda e: (
            e.tensor_reduce(lamv[:, 0:1], lamr[:, 0:D], axis=AX.X, op=OP.add),
            e.tensor_reduce(lamv[:, 1:2], lamr[:, 2 * D:3 * D], axis=AX.X, op=OP.add))[-1],
            wait=[t1_], inc="v")
        t3_ = S.step("scalar", lambda e: (
            e.activation(lamv[:, 2:3], lamv[:, 0:1], AF.Exp),
            e.activation(lamv[:, 3:4], lamv[:, 1:2], AF.Exp))[-1],
            wait=[t2_], inc="s")
        t4_ = S.step("vector", lambda e: e.tensor_tensor(lamv[:, 4:5], lamv[:, 2:3],
                                                         lamv[:, 3:4], op=OP.subtract),
                     wait=[t3_], inc="v")
        t5_ = S.step("vector", lambda e: e.tensor_scalar(
            lamv[:, 5:6], lamv[:, 4:5], float(cfg["LAM_INIT"]), -1.0,
            op0=OP.add, op1=OP.mult), wait=[t4_], inc="v")
        LAM = S.step("vector", lambda e: e.tensor_scalar(
            lamv[:, 4:5], lamv[:, 4:5], float(cfg["LAM_INIT"]), None, op0=OP.add),
            wait=[t5_], inc="v")

        consts = dict(identf=identf, identb=identb, nib=nib, olnb=olnb, ob32=ob32,
                      ebr=ebr, lamv=lamv, epsc=epsc, CONST=CONST, LAM=LAM)

        for b in range(B):
            emit_batch(nc, S, cfg, b, dram, xt_scr, dT_scr, y_out, consts)

        fin = []
        for s_ in ["o0", "o1"]:
            if S.counts.get(s_, 0):
                fin.append((s_, S.counts[s_]))
        S.step("sync", lambda e: e.nop(), wait=fin)
        blk = top.enter_context(nc.Block())
        S.emit(blk, sems)
    return nc


def emit_batch(nc, S, cfg, b, dram, xt_scr, dT_scr, y_out, K):
    B, T, C, CH = cfg["B"], cfg["T"], cfg["C"], cfg["CH"]
    NH, NH2, D, DV = cfg["NH"], cfg["NH2"], cfg["D"], cfg["DV"]
    E, I, ISH, EG, EPG = cfg["E"], cfg["I"], cfg["ISH"], cfg["EG"], cfg["EPG"]
    IC, ICS = cfg["IC"], cfg["ICS"]
    TG, NTG, NTT, QPG = cfg["TG"], cfg["NTG"], cfg["NTT"], cfg["QPG"]
    NSQ, HPSQ = cfg["NSQ"], cfg["HPSQ"]
    SMS, EPS = cfg["SM_SCALE"], cfg["EPS"]
    nch = NH2 * D // 128
    ICH = ICS // 2
    identf, identb = K["identf"], K["identb"]
    nib, olnb, ob32, ebr, lamv = K["nib"], K["olnb"], K["ob32"], K["ebr"], K["lamv"]
    epsc = K["epsc"]
    CONST, LAM = K["CONST"], K["LAM"]

    with ExitStack() as bs:
        s_x = ExitStack()
        xbf = s_x.enter_context(nc.sbuf_tensor(f"xbf{b}", [128, CH * T], BF16))

        def xbf_sl(c, t0, n):
            return xbf[:, c * T + t0: c * T + t0 + n]

        with ExitStack() as p0:
            xt = [p0.enter_context(nc.sbuf_tensor(f"p0xt{b}_{i}", [128, C], F32)) for i in range(2)]
            xstg = [p0.enter_context(nc.sbuf_tensor(f"p0st{b}_{i}", [128, C], F32)) for i in range(2)]
            pt = [p0.enter_context(nc.psum_tensor(f"p0ps{b}_{i}", [128, 512], F32)) for i in range(2)]
            xt_free = [None, None]
            stg_free = [None, None]
            BAR_B = S.barrier()
            for tt in range(NTT):
                sl = tt % 2
                dxn = f"dx{sl}"
                d_tok = S.dma("sync", xt[sl][:], dram["x"][b, 128 * tt:128 * tt + 128, :],
                              wait=lt(xt_free[sl], BAR_B if tt < 2 else None), inc=dxn)
                ev_toks = []
                tr_toks = []
                for half_c in range(2):
                    cg = 4 * half_c
                    psl = pt[half_c]
                    def trs(e, sl=sl, cg=cg, psl=psl):
                        last = None
                        for i_ in range(4):
                            last = e.transpose(psl[:, 128 * i_:128 * i_ + 128],
                                               xt[sl][:, 128 * (cg + i_):128 * (cg + i_) + 128],
                                               identf[:])
                        return last
                    tr = S.step("tensor", trs,
                                wait=lt([d_tok], CONST, ev_toks[-1] if ev_toks else None,
                                        stg_free[sl]), inc="t")
                    ev = S.step("vector",
                                lambda e, sl=sl, cg=cg, psl=psl: e.tensor_copy(
                                    xstg[sl][:, 128 * cg:128 * cg + 512], psl[:, 0:512]),
                                wait=[tr], inc="v")
                    tr_toks.append(tr)
                    ev_toks.append(ev)
                xt_free[sl] = tr_toks[-1]
                EV = lt(*ev_toks)
                sc = S.step("scalar",
                            lambda e, sl=sl, tt=tt: e.copy(
                                xbf[:].rearrange("p (c t) -> p c t", c=CH)[:, :, 128 * tt:128 * tt + 128],
                                xstg[sl][:].rearrange("p (c n) -> p c n", c=CH)),
                            wait=EV, inc="s")
                dsc = S.dma("sync",
                            xt_scr[b].rearrange("(c p) t -> p c t", p=128)[:, :, 128 * tt:128 * tt + 128],
                            xstg[sl][:].rearrange("p (c n) -> p c n", c=CH),
                            wait=EV, inc=f"dsc{sl}")
                stg_free[sl] = lt(sc, dsc)
            XBF_READY = lt(("s", S.counts["s"]))
            XSCR_DONE = lt(("dsc0", S.counts.get("dsc0", 0)), ("dsc1", S.counts.get("dsc1", 0)))

        with ExitStack() as p1:
            cosb = p1.enter_context(nc.sbuf_tensor(f"cosb{b}", [128, T], BF16))
            sinb = p1.enter_context(nc.sbuf_tensor(f"sinb{b}", [128, T], BF16))
            p1b = p1.enter_context(nc.sbuf_tensor(f"p1b{b}", [128, nch * 128], BF16))
            p2b = p1.enter_context(nc.sbuf_tensor(f"p2b{b}", [128, nch * 128], BF16))
            cmb = p1.enter_context(nc.sbuf_tensor(f"cmb{b}", [128, QPG * TG], BF16))
            wA = p1.enter_context(nc.sbuf_tensor(f"wA{b}", [128, 12 * CH * 128], BF16))
            qT = p1.enter_context(nc.sbuf_tensor(f"qT{b}", [128, 4 * T], BF16))
            kT = p1.enter_context(nc.sbuf_tensor(f"kT{b}", [128, 4 * T], BF16))
            vsb = p1.enter_context(nc.sbuf_tensor(f"vsb{b}", [128, NTT * 520], BF16))
            rt = [p1.enter_context(nc.sbuf_tensor(f"rt{b}_{i}", [128, TG], BF16)) for i in range(4)]
            rqa = p1.enter_context(nc.sbuf_tensor(f"rqa{b}", [128, TG], BF16))
            rqb = p1.enter_context(nc.sbuf_tensor(f"rqb{b}", [128, TG], BF16))
            rox = [p1.enter_context(nc.sbuf_tensor(f"rox{b}_{i}", [128, TG], BF16)) for i in range(2)]
            Pb2 = [[p1.enter_context(nc.sbuf_tensor(f"Pb{b}_{i}_{j}", [128, T], BF16)) for j in range(2)] for i in range(2)]
            PTs = [p1.enter_context(nc.sbuf_tensor(f"PTs{b}_{i}", [128, 1024], BF16)) for i in range(3)]
            mb = p1.enter_context(nc.sbuf_tensor(f"mb{b}", [128, 32], F32))
            biasK = p1.enter_context(nc.sbuf_tensor(f"biasK{b}", [128, 32], F32))
            fbk = p1.enter_context(nc.sbuf_tensor(f"fbk{b}", [128, 32], F32))
            bcol = p1.enter_context(nc.sbuf_tensor(f"bcol{b}", [128, 8], F32))
            dsc_ = p1.enter_context(nc.sbuf_tensor(f"dscr{b}", [128, 16], F32))
            dbuf = p1.enter_context(nc.sbuf_tensor(f"dbuf{b}", [128, DV], F32))
            dbuf2 = p1.enter_context(nc.sbuf_tensor(f"dbuf2{b}", [128, DV], F32))
            diff2 = p1.enter_context(nc.sbuf_tensor(f"diff2{b}", [128, NTT * 128], BF16))
            dstg = [p1.enter_context(nc.sbuf_tensor(f"dstg{b}_{i}", [128, 1024], BF16)) for i in range(2)]

            wA_free = None
            BAR_P1 = S.barrier()
            S.dma("gpsimd", cosb[:], dram["cosr"], wait=BAR_P1, inc="dwa")
            S.dma("gpsimd", sinb[:], dram["sinr"], inc="dwa")
            S.dma("gpsimd", p1b[:], dram["permx1"], inc="dwa")
            S.dma("gpsimd", p2b[:], dram["permx2"], inc="dwa")
            S.dma("gpsimd", cmb[:], dram["cmask"], inc="dwa")
            TABS = lt(("dwa", S.counts["dwa"]))
            DIFF_EV = []
            dstg_free = [None, None]
            for SQ in range(NSQ):
                if NSQ > 1:
                    qcc = [SQ, 2 + SQ, 4 + SQ, 6 + SQ]
                    vc0 = 2 * nch * 128 + 512 * SQ
                else:
                    qcc = [0, 1, 2, 3]
                    vc0 = 2 * nch * 128
                col0 = [128 * cc for cc in qcc] + [nch * 128 + 128 * cc for cc in qcc] + \
                       [vc0 + 128 * j for j in range(4)]
                for j, c0 in enumerate(col0):
                    S.dma("gpsimd",
                          wA[:].rearrange("p (j c n) -> p j c n", j=12, c=CH)[:, j],
                          dram["wattn"][:, c0:c0 + 128].rearrange("(c p) n -> p c n", p=128),
                          wait=lt(wA_free, BAR_P1 if SQ == 0 else None), inc="dwa",
                          tag=f"wa{b}_{SQ}_{j}")
                WA = lt(("dwa", S.counts["dwa"]))

                def wa_tile(blk, c):
                    return wA[:, (blk * CH + c) * 128:(blk * CH + c) * 128 + 128]

                with ExitStack() as pq:
                    pAs = [pq.enter_context(nc.psum_tensor(f"pA{b}{SQ}_{i}", [128, 512], F32)) for i in range(2)]
                    pBs = [pq.enter_context(nc.psum_tensor(f"pB{b}{SQ}_{i}", [128, 512], F32)) for i in range(2)]
                    pP = pq.enter_context(nc.psum_tensor(f"pP{b}{SQ}", [128, 512], F32))
                    pVs = [pq.enter_context(nc.psum_tensor(f"pV{b}{SQ}_{i}", [128, 512], F32)) for i in range(2)]
                    pA_free, pB_free = [None, None], [None, None]
                    pP_free = None
                    pV_free = [None, None]
                    rqa_free, rqb_free = None, None
                    rt_free = [None] * 4
                    rox_free = [None, None]
                    for ti, dst in ((0, qT), (1, kT)):
                        for pr in range(2):
                            x1b, x2b = 4 * ti + pr, 4 * ti + pr + 2
                            for tg in range(NTG):
                                qsl = tg % 2
                                pA, pB = pAs[qsl], pBs[qsl]
                                def mmA(e, x1b=x1b, tg=tg, pA=pA):
                                    last = None
                                    for c in range(CH):
                                        last = e.matmul(pA[:, 0:TG], wa_tile(x1b, c),
                                                        xbf_sl(c, TG * tg, TG),
                                                        start=(c == 0), stop=(c == CH - 1))
                                    return last
                                tA = S.step("tensor", mmA, wait=lt(WA, XBF_READY, pA_free[qsl]), inc="t")
                                def mmB(e, x2b=x2b, tg=tg, pB=pB):
                                    last = None
                                    for c in range(CH):
                                        last = e.matmul(pB[:, 0:TG], wa_tile(x2b, c),
                                                        xbf_sl(c, TG * tg, TG),
                                                        start=(c == 0), stop=(c == CH - 1))
                                    return last
                                tB = S.step("tensor", mmB, wait=lt(pB_free[qsl]), inc="t")
                                evA = S.step("scalar", lambda e, pA=pA: e.copy(
                                    rqa[:], pA[:, 0:TG]), wait=lt([tA], rqa_free), inc="s")
                                evB = S.step("scalar", lambda e, pB=pB: e.copy(
                                    rqb[:], pB[:, 0:TG]), wait=lt([tB], rqb_free), inc="s")
                                pA_free[qsl] = evA
                                pB_free[qsl] = evB
                                v1 = S.step("vector", lambda e, tg=tg: e.tensor_tensor(
                                    rt[0][:], rqa[:], cosb[:, TG * tg:TG * tg + TG], op=OP.mult),
                                    wait=lt([evA], rt_free[0]), inc="v")
                                v2 = S.step("gpsimd", lambda e, tg=tg: e.tensor_tensor(
                                    rt[1][:], rqb[:], sinb[:, TG * tg:TG * tg + TG], op=OP.mult),
                                    wait=lt([evB], rt_free[1]), inc="p")
                                v3 = S.step("vector", lambda e, tg=tg: e.tensor_tensor(
                                    rt[2][:], rqa[:], sinb[:, TG * tg:TG * tg + TG], op=OP.mult),
                                    wait=lt([evA], rt_free[2], [v1]), inc="v")
                                v4 = S.step("gpsimd", lambda e, tg=tg: e.tensor_tensor(
                                    rt[3][:], rqb[:], cosb[:, TG * tg:TG * tg + TG], op=OP.mult),
                                    wait=lt([evB], rt_free[3], [v2]), inc="p")
                                rqa_free = lt(v1, v3)
                                rqb_free = lt(v2, v4)
                                v5 = S.step("vector", lambda e: e.tensor_tensor(
                                    rox[0][:], rt[0][:], rt[1][:], op=OP.subtract),
                                    wait=lt(v1, v2, rox_free[0]), inc="v")
                                v6 = S.step("gpsimd", lambda e: e.tensor_tensor(
                                    rox[1][:], rt[2][:], rt[3][:], op=OP.add),
                                    wait=lt(v3, v4, rox_free[1]), inc="p")
                                rt_free = [v5, v5, v6, v6]
                                ev_toks = []
                                for k2 in range(2):
                                    if NSQ > 1:
                                        gc = (2 * SQ + k2) if pr == 0 else (4 + 2 * SQ + k2)
                                        lc = k2 if pr == 0 else 2 + k2
                                    else:
                                        gc = 2 * pr + k2
                                        lc = gc
                                    def pmm(e, gc=gc):
                                        e.matmul(pP[:, 0:TG], p1b[:, 128 * gc:128 * gc + 128],
                                                 rox[0][:], start=True, stop=False)
                                        return e.matmul(pP[:, 0:TG], p2b[:, 128 * gc:128 * gc + 128],
                                                        rox[1][:], start=False, stop=True)
                                    tP = S.step("tensor", pmm, wait=lt(v5, v6, pP_free), inc="t")
                                    evP = S.step("scalar",
                                                 lambda e, dst=dst, lc=lc, tg=tg: e.copy(
                                                     dst[:, lc * T + TG * tg: lc * T + TG * tg + TG],
                                                     pP[:, 0:TG]),
                                                 wait=[tP], inc="s")
                                    pP_free = evP
                                    ev_toks.append(evP)
                                rox_free = [lt(*ev_toks), lt(*ev_toks)]
                    QK_READY = lt(("s", S.counts["s"]))

                    wa3 = wA[:].rearrange("p (j c n) -> p j c n", j=12, c=CH)
                    for tt in range(NTT):
                        vsl = tt % 2
                        pV = pVs[vsl]
                        def vmm(e, tt=tt, pV=pV):
                            last = None
                            for c in range(CH):
                                rhs = wa3[:, 8:12, c, :]
                                last = e.matmul(pV[:, 0:512], xbf_sl(c, 128 * tt, 128), rhs,
                                                start=(c == 0), stop=(c == CH - 1))
                            return last
                        tV = S.step("tensor", vmm, wait=lt(WA, XBF_READY, pV_free[vsl]), inc="t")
                        evV = S.step("vector",
                                     lambda e, tt=tt, pV=pV: e.tensor_copy(
                                         vsb[:, 520 * tt:520 * tt + 520].rearrange(
                                             "p (h n) -> p h n", h=8)[:, :, 0:64],
                                         pV[:, 0:512].rearrange("p (h n) -> p h n", h=8)),
                                     wait=[tV], inc="v")
                        msV = S.step("vector",
                                     lambda e, tt=tt: e.memset(
                                         vsb[:, 520 * tt:520 * tt + 520].rearrange(
                                             "p (h n) -> p h n", h=8)[:, :, 64:65], 1.0),
                                     wait=[tV], inc="v")
                        pV_free[vsl] = evV
                    V_READY = lt(("v", S.counts["v"]))

                with ExitStack() as ps_:
                    psS = [ps_.enter_context(nc.psum_tensor(f"psS{b}{SQ}_{i}", [128, 512], F32)) for i in range(3)]
                    psPT = [ps_.enter_context(nc.psum_tensor(f"psPT{b}{SQ}_{i}", [128, 1024], BF16)) for i in range(3)]
                    psA = [ps_.enter_context(nc.psum_tensor(f"psA{b}{SQ}_{i}", [128, 512], F32)) for i in range(2)]
                    sS_free = [None] * 3
                    sPT_free = [None] * 3
                    sA_free = [None, None]
                    PTs_free = [None] * 3
                    sctr = [0]
                    tctr = [0]
                    Pb_free = [[None, None], [None, None]]
                    mb_free = [None] * 4
                    bcol_free = [None] * 8
                    dsc_free = None
                    d2w = [[None, None] for _ in range(NTT)]
                    d2r = None
                    biasTok = [[None, None], [None, None]]
                    expTok = [[None, None], [None, None]]
                    fbk_free = [None] * 4

                    items = [(lhp, qt) for lhp in range(HPSQ) for qt in range(NTT)]

                    def hp_info(lhp):
                        hp = HPSQ * SQ + lhp
                        base = 32 * (hp % 4)
                        lc_a = (hp // 4 - 2 * SQ) if NSQ > 1 else hp // 4
                        return hp, base, lc_a, lc_a + 2

                    def stage_AB_gen(i):
                        lhp, qt = items[i]
                        hp, base, lc_a, lc_a2 = hp_info(lhp)
                        par = i % 2
                        nkg = qt // QPG + 1
                        for hi, lc in ((0, lc_a), (1, lc_a2)):
                            idx8 = (2 * hi + par) * 8
                            e_toks = []
                            red_toks = []
                            for kg in range(nkg):
                                bk = sctr[0] % 3
                                sctr[0] += 1
                                def smm(e, lc=lc, qt=qt, kg=kg, bk=bk, base=base):
                                    return e.matmul(
                                        psS[bk][:, 0:TG],
                                        qT[base:base + 32, lc * T + 128 * qt: lc * T + 128 * qt + 128],
                                        kT[base:base + 32, lc * T + TG * kg: lc * T + TG * kg + TG],
                                        start=True, stop=True, tile_position=(base, 0))
                                tS = S.step("tensor", smm, wait=lt(QK_READY, sS_free[bk]), inc="t")
                                yield
                                rd = S.step("vector",
                                            lambda e, kg=kg, bk=bk, idx8=idx8: e.reduce_max(
                                                mb[:, idx8 + kg:idx8 + kg + 1],
                                                psS[bk][:, 0:TG], axis=AX.X, negate=True),
                                            wait=lt([tS], mb_free[2 * hi + par]
                                                    if kg == 0 else None), inc="v")
                                bs_ = S.step("vector",
                                             lambda e, kg=kg, idx8=idx8: e.tensor_scalar_mul(
                                                 biasK[:, idx8 + kg:idx8 + kg + 1],
                                                 mb[:, idx8 + kg:idx8 + kg + 1], float(SMS)),
                                             wait=[rd], inc="v")
                                ex = S.step("scalar",
                                            lambda e, hi=hi, par=par, kg=kg, bk=bk, idx8=idx8:
                                            e.activation(
                                                Pb2[hi][par][:, TG * kg:TG * kg + TG],
                                                psS[bk][:, 0:TG],
                                                AF.Exp, bias=biasK[:, idx8 + kg:idx8 + kg + 1],
                                                scale=float(SMS)),
                                            wait=lt([bs_], Pb_free[hi][par]), inc="s")
                                sS_free[bk] = ex
                                e_toks.append(ex)
                                red_toks.append(rd)
                            bi = 2 * hi + par
                            cm1 = S.step("vector",
                                         lambda e, nkg=nkg, bi=bi, idx8=idx8: e.tensor_reduce(
                                             bcol[:, bi:bi + 1], mb[:, idx8:idx8 + nkg],
                                             axis=AX.X, op=OP.min),
                                         wait=lt(*red_toks, bcol_free[bi]), inc="v")
                            mb_free[bi] = cm1
                            cm2 = S.step("vector",
                                         lambda e, bi=bi: e.tensor_scalar_mul(
                                             bcol[:, 4 + bi:5 + bi], bcol[:, bi:bi + 1], float(SMS)),
                                         wait=lt([cm1], bcol_free[4 + bi]), inc="v")
                            biasTok[hi][par] = cm2
                            if nkg > 1:
                                fb1 = S.step("vector",
                                             lambda e, idx8=idx8, nkg=nkg, bi=bi: e.tensor_scalar(
                                                 fbk[:, idx8:idx8 + nkg], biasK[:, idx8:idx8 + nkg],
                                                 bcol[:, 4 + bi:5 + bi], None, op0=OP.subtract),
                                             wait=lt([cm2], fbk_free[bi]), inc="v")
                                fb2 = S.step("scalar",
                                             lambda e, idx8=idx8, nkg=nkg: e.activation(
                                                 fbk[:, idx8:idx8 + nkg], fbk[:, idx8:idx8 + nkg],
                                                 AF.Exp, scale=-1.0),
                                             wait=[fb1], inc="s")
                                bcol_free[4 + bi] = fb1
                                rs_toks = []
                                for kg in range(nkg):
                                    rs = S.step("gpsimd",
                                                lambda e, hi=hi, par=par, kg=kg, idx8=idx8:
                                                e.tensor_tensor(
                                                    Pb2[hi][par][:, TG * kg:TG * kg + TG],
                                                    Pb2[hi][par][:, TG * kg:TG * kg + TG],
                                                    fbk[:, idx8 + kg:idx8 + kg + 1]
                                                    .to_broadcast([128, TG]),
                                                    op=OP.mult),
                                                wait=lt([fb2], e_toks[kg]), inc="p")
                                    rs_toks.append(rs)
                                mk = S.step("gpsimd",
                                            lambda e, hi=hi, par=par, qt=qt, nkg=nkg: e.tensor_tensor(
                                                Pb2[hi][par][:, TG * (nkg - 1):TG * nkg],
                                                Pb2[hi][par][:, TG * (nkg - 1):TG * nkg],
                                                cmb[:, TG * (qt % QPG):TG * (qt % QPG) + TG],
                                                op=OP.mult),
                                            wait=lt(rs_toks[-1]), inc="p")
                                fbk_free[bi] = lt(mk, *rs_toks)
                                expTok[hi][par] = lt(mk, *rs_toks)
                            else:
                                bcol_free[4 + bi] = cm2
                                mk = S.step("gpsimd",
                                            lambda e, hi=hi, par=par, qt=qt, nkg=nkg: e.tensor_tensor(
                                                Pb2[hi][par][:, TG * (nkg - 1):TG * nkg],
                                                Pb2[hi][par][:, TG * (nkg - 1):TG * nkg],
                                                cmb[:, TG * (qt % QPG):TG * (qt % QPG) + TG],
                                                op=OP.mult),
                                            wait=lt(*e_toks), inc="p")
                                expTok[hi][par] = lt(mk, *e_toks)

                    def stage_C_gen(i):
                        nonlocal dsc_free, d2r
                        lhp, qt = items[i]
                        hp, base, lc_a, lc_a2 = hp_info(lhp)
                        par = i % 2
                        aslot = i % 2
                        nkt = qt + 1
                        nkg = qt // QPG + 1
                        vh = lhp
                        pv_last = []
                        batches = []
                        for hi in (0, 1):
                            for k0 in range(0, nkt, 8):
                                batches.append((hi, list(range(k0, min(k0 + 8, nkt)))))
                        pend_pv = []
                        pv_tok = {0: None, 1: None}
                        first_pv = {0: True, 1: True}
                        last_tT = {0: None, 1: None}

                        def emit_trp(hi, kts):
                            tsl = tctr[0] % 3
                            tctr[0] += 1
                            ealt = tctr[0] % 2
                            def trp(e, hi=hi, par=par, kts=tuple(kts), tsl=tsl):
                                last = None
                                for i_, kt in enumerate(kts):
                                    last = e.transpose(
                                        psPT[tsl][:, 128 * i_:128 * i_ + 128],
                                        Pb2[hi][par][:, 128 * kt:128 * kt + 128], identb[:])
                                return last
                            tT = S.step("tensor", trp,
                                        wait=lt(expTok[hi][par], sPT_free[tsl]), inc="t")
                            last_tT[hi] = tT
                            if ealt == 0:
                                evT = S.step("scalar",
                                             lambda e, tsl=tsl, n=len(kts): e.copy(
                                                 PTs[tsl][:, 0:128 * n], psPT[tsl][:, 0:128 * n]),
                                             wait=lt([tT], PTs_free[tsl]), inc="s")
                            else:
                                evT = S.step("vector",
                                             lambda e, tsl=tsl, n=len(kts): e.tensor_copy(
                                                 PTs[tsl][:, 0:128 * n], psPT[tsl][:, 0:128 * n]),
                                             wait=lt([tT], PTs_free[tsl]), inc="v")
                            sPT_free[tsl] = evT
                            return evT, tsl

                        def emit_pv(hi, kts, evT, tsl):
                            def pvm(e, kts=tuple(kts), tsl=tsl, vh=vh, hi=hi,
                                    aslot=aslot, first=first_pv[hi], nkt=nkt):
                                last = None
                                for i_, kt in enumerate(kts):
                                    last = e.matmul(
                                        psA[aslot][:, 65 * hi:65 * hi + 65],
                                        PTs[tsl][:, 128 * i_:128 * i_ + 128],
                                        vsb[:, 520 * kt + 65 * vh:520 * kt + 65 * vh + 65],
                                        start=(first and i_ == 0),
                                        stop=(kt == nkt - 1))
                                return last
                            tPV = S.step("tensor", pvm,
                                         wait=lt([evT], V_READY,
                                                 sA_free[aslot] if first_pv[hi] else None),
                                         inc="t")
                            PTs_free[tsl] = tPV
                            first_pv[hi] = False
                            pv_tok[hi] = tPV

                        pend_pv = []
                        bi_ = 0
                        while bi_ < len(batches) or pend_pv:
                            if bi_ < len(batches) and len(pend_pv) < 2:
                                hi, kts = batches[bi_]
                                evT, tsl = emit_trp(hi, kts)
                                pend_pv.append((hi, kts, evT, tsl))
                                bi_ += 1
                            else:
                                hi, kts, evT, tsl = pend_pv.pop(0)
                                emit_pv(hi, kts, evT, tsl)
                            yield
                        for hi in (0, 1):
                            Pb_free[hi][par] = last_tT[hi]
                            pv_last.append(pv_tok[hi])
                        PVD = lt(*pv_last)

                        c1 = S.step("vector", lambda e, aslot=aslot: (
                            e.reciprocal(dsc_[:, 0:1], psA[aslot][:, 64:65]),
                            e.reciprocal(dsc_[:, 1:2], psA[aslot][:, 129:130]))[-1],
                            wait=lt(PVD, LAM, dsc_free), inc="v")
                        c2 = S.step("vector", lambda e, aslot=aslot: (
                            e.tensor_tensor(dsc_[:, 2:3], dsc_[:, 1:2], lamv[:, 5:6], op=OP.mult),
                            e.tensor_scalar(dbuf[:], psA[aslot][:, 0:64], dsc_[:, 0:1], None,
                                            op0=OP.mult))[-1],
                            wait=[c1], inc="v")
                        c3 = S.step("vector", lambda e, aslot=aslot: e.scalar_tensor_tensor(
                            dbuf2[:], psA[aslot][:, 65:129], dsc_[:, 2:3], dbuf[:],
                            op0=OP.mult, op1=OP.add),
                            wait=[c2], inc="v")
                        c4 = S.step("scalar", lambda e: e.activation(
                            dbuf[:], dbuf2[:], AF.Square, accum_out=dsc_[:, 4:5]),
                            wait=[c3], inc="s")
                        c7a = S.step("scalar", lambda e: e.activation(
                            dsc_[:, 5:6], dsc_[:, 4:5], AF.Ln, scale=1.0 / DV, bias=epsc[:]),
                            wait=[c4], inc="s")
                        c7 = S.step("scalar", lambda e: e.activation(
                            dsc_[:, 6:7], dsc_[:, 5:6], AF.Exp, scale=-0.5),
                            wait=[c7a], inc="s")
                        sA_free[aslot] = c3
                        c9 = S.step("vector",
                                    lambda e, qt=qt, lhp=lhp: e.tensor_scalar(
                                        diff2[:, 128 * qt + 64 * (lhp % 2):
                                              128 * qt + 64 * (lhp % 2) + 64],
                                        dbuf2[:], dsc_[:, 6:7], float(1.0 - cfg["LAM_INIT"]),
                                        op0=OP.mult, op1=OP.mult),
                                    wait=lt([c7], d2r), inc="v")
                        dsc_free = c9
                        d2w[qt][lhp % 2] = c9

                        if lhp % 2 == 1 and qt == NTT - 1:
                            gchunk = hp // 2
                            last_tD = None
                            for t0 in range(0, NTT, 8):
                                n = min(8, NTT - t0)
                                tsl = tctr[0] % 3
                                tctr[0] += 1
                                def dtr(e, t0=t0, n=n, tsl=tsl):
                                    last = None
                                    for i_ in range(n):
                                        last = e.transpose(
                                            psPT[tsl][:, 128 * i_:128 * i_ + 128],
                                            diff2[:, 128 * (t0 + i_):128 * (t0 + i_) + 128],
                                            identb[:])
                                    return last
                                dwaits = lt(*[tok for i_ in range(n) for tok in d2w[t0 + i_]],
                                            sPT_free[tsl])
                                tD = S.step("tensor", dtr, wait=dwaits, inc="t")
                                dsl = (t0 // 8) % 2
                                evD = S.step("vector",
                                             lambda e, n=n, tsl=tsl, dsl=dsl: e.tensor_copy(
                                                 dstg[dsl][:, 0:128 * n], psPT[tsl][:, 0:128 * n]),
                                             wait=lt([tD], dstg_free[dsl]), inc="v")
                                sPT_free[tsl] = evD
                                ddma = S.dma("sync",
                                             dT_scr[b].rearrange("(c p) t -> p c t", p=128)
                                             [:, gchunk, 128 * t0:128 * t0 + 128 * n],
                                             dstg[dsl][:, 0:128 * n],
                                             wait=[evD], inc=f"dd{dsl}")
                                dstg_free[dsl] = ddma
                                DIFF_EV.append(ddma)
                                last_tD = tD
                            d2r = last_tD

                    NI = len(items)
                    def drain(g):
                        if g is None:
                            return None
                        try:
                            next(g)
                            return g
                        except StopIteration:
                            return None
                    for i in range(NI + 1):
                        g_ab = stage_AB_gen(i) if i < NI else None
                        g_c = stage_C_gen(i - 1) if i >= 1 else None
                        while g_ab is not None or g_c is not None:
                            g_c = drain(g_c)
                            g_c = drain(g_c)
                            g_ab = drain(g_ab)
                wA_free = lt(("t", S.counts["t"]))
            DIFFT_READY = lt(("dd0", S.counts.get("dd0", 0)), ("dd1", S.counts.get("dd1", 0)))
        s_x.close()

        y32 = bs.enter_context(nc.sbuf_tensor(f"y32{b}", [128, CH * T], F32))
        ybf = bs.enter_context(nc.sbuf_tensor(f"ybf{b}", [128, CH * T], BF16))

        def y32_sl(c, t0, n):
            return y32[:, c * T + t0: c * T + t0 + n]

        def ybf_sl(c, t0, n):
            return ybf[:, c * T + t0: c * T + t0 + n]

        with ExitStack() as p2:
            wp = p2.enter_context(nc.sbuf_tensor(f"wp{b}", [128, CH * C], BF16))
            xts = p2.enter_context(nc.sbuf_tensor(f"xts{b}", [128, CH * TG], F32))
            dts = [p2.enter_context(nc.sbuf_tensor(f"dts{b}_{i}", [128, CH * TG], BF16)) for i in range(2)]
            lnb = alloc_ln_bufs(nc, p2, b, 0, TG)
            pat = [p2.enter_context(nc.psum_tensor(f"pat{b}_{i}", [128, 512], F32)) for i in range(2)]
            pstA = p2.enter_context(nc.psum_tensor(f"pstA{b}", [128, 512], F32))
            pstB = p2.enter_context(nc.psum_tensor(f"pstB{b}", [128, 512], F32))
            pbc = p2.enter_context(nc.psum_tensor(f"pbc{b}", [128, 1024], F32))

            BAR_P2 = S.barrier()
            S.dma("gpsimd", wp[:].rearrange("p (c n) -> p c n", c=CH),
                  dram["wproj"].rearrange("(c p) n -> p c n", p=128),
                  wait=lt(CONST, BAR_P2), inc="dwa")
            WP = lt(("dwa", S.counts["dwa"]))
            xts_free = None
            dts_free = [None, None]
            pat_free = [None, None]
            ln_state = dict(prev=None)
            for tg in range(NTG):
                dsl2 = tg % 2
                ddt = S.dma("sync",
                            dts[dsl2][:].rearrange("p (c n) -> p c n", c=CH),
                            dT_scr[b].rearrange("(c p) t -> p c t", p=128)[:, :, TG * tg:TG * tg + TG],
                            wait=lt(DIFFT_READY, dts_free[dsl2], BAR_P2 if tg < 2 else None),
                            inc=f"dd{dsl2}")
                dxt = S.dma("sync",
                            xts[:].rearrange("p (c n) -> p c n", c=CH),
                            xt_scr[b].rearrange("(c p) t -> p c t", p=128)[:, :, TG * tg:TG * tg + TG],
                            wait=lt(XSCR_DONE, xts_free, BAR_P2 if tg == 0 else None), inc="dxs")
                xa_toks = []
                for cc in range(CH):
                    bsl = cc % 2
                    def wpm(e, cc=cc, dsl2=dsl2, bsl=bsl):
                        last = None
                        for c in range(CH):
                            last = e.matmul(pat[bsl][:, 0:TG],
                                            wp[:, c * C + 128 * cc: c * C + 128 * cc + 128],
                                            dts[dsl2][:, c * TG:c * TG + TG],
                                            start=(c == 0), stop=(c == CH - 1))
                        return last
                    tw = S.step("tensor", wpm, wait=lt(WP, [ddt], pat_free[bsl]), inc="t")
                    xa = S.step("vector",
                                lambda e, cc=cc, bsl=bsl: e.tensor_tensor(
                                    xts[:, cc * TG:cc * TG + TG], xts[:, cc * TG:cc * TG + TG],
                                    pat[bsl][:, 0:TG], op=OP.add),
                                wait=lt([tw], [dxt]), inc="v")
                    pat_free[bsl] = xa
                    xa_toks.append(xa)
                    dts_free[dsl2] = tw
                ln_last = emit_ln(nc, S, cfg, tg, lambda c, tg=tg: xts[:, c * TG:c * TG + TG],
                                  y32_sl, ybf_sl, lnb, pstA, pstB, pbc, olnb, ob32,
                                  xa_toks, ln_state)
                xts_free = ln_last
            Y_READY = S.barrier()

        s_f = ExitStack()
        coefT = s_f.enter_context(nc.sbuf_tensor(f"coefT{b}", [64, T], BF16))
        sg_ = [s_f.enter_context(nc.sbuf_tensor(f"sgb{b}_{i}", [128, TG], F32)) for i in range(2)]
        tm1 = [s_f.enter_context(nc.sbuf_tensor(f"tm1{b}_{i}", [128, TG], BF16)) for i in range(2)]
        selb = s_f.enter_context(nc.sbuf_tensor(f"selb{b}", [64, 32 * 128], BF16))
        wg = [s_f.enter_context(nc.sbuf_tensor(f"wg{b}_{i}", [128, CH * I], BF16)) for i in range(2)]
        wu = [s_f.enter_context(nc.sbuf_tensor(f"wu{b}_{i}", [128, CH * I], BF16)) for i in range(2)]
        wd = [s_f.enter_context(nc.sbuf_tensor(f"wd{b}_{i}", [128, IC * C], BF16)) for i in range(2)]
        hT = s_f.enter_context(nc.sbuf_tensor(f"hT{b}", [128, 2 * IC * TG], BF16))
        coefb = [s_f.enter_context(nc.sbuf_tensor(f"coefb{b}_{i}", [128, TG], BF16)) for i in range(2)]
        pgu = [s_f.enter_context(nc.psum_tensor(f"pgu{b}_{i}", [128, 512], F32)) for i in range(4)]

        NEXP = E + 2

        def wsrc(e_i, which):
            if e_i < E:
                return dram[which][e_i]
            h_ = e_i - E
            if which == "Wg":
                return dram["SWg"][:, I * h_:I * (h_ + 1)]
            if which == "Wu":
                return dram["SWu"][:, I * h_:I * (h_ + 1)]
            return dram["SWd"][I * h_:I * (h_ + 1), :]

        BAR_F = S.barrier()
        ct_init = S.step("vector", lambda e: e.memset(coefT[:], 0.0), wait=BAR_F, inc="v")
        S.dma("gpsimd", selb[:], dram["sel_bank"], wait=BAR_F, inc="dwa")
        SEL = lt(("dwa", S.counts["dwa"]))
        wload_g = [None, None]
        wload_u = [None, None]
        wload_d = [None, None]

        def emit_wdma(e_i, wait=None):
            par = e_i % 2
            t1 = S.dma("gpsimd", wg[par][:].rearrange("p (c n) -> p c n", c=CH),
                       wsrc(e_i, "Wg").rearrange("(c p) n -> p c n", p=128),
                       wait=lt(wait), inc=f"dwg{par}")
            t2 = S.dma("gpsimd", wu[par][:].rearrange("p (c n) -> p c n", c=CH),
                       wsrc(e_i, "Wu").rearrange("(c p) n -> p c n", p=128), inc=f"dwu{par}")
            t3 = S.dma("gpsimd", wd[par][:].rearrange("p (c n) -> p c n", c=IC),
                       wsrc(e_i, "Wd").rearrange("(c p) n -> p c n", p=128), inc=f"dwd{par}")
            wload_g[par], wload_u[par], wload_d[par] = t1, t2, t3

        emit_wdma(0, wait=BAR_F)
        emit_wdma(1, wait=BAR_F)

        with ExitStack() as p3:
            rws = p3.enter_context(nc.sbuf_tensor(f"rws{b}", [E, C], F32))
            rwTh = p3.enter_context(nc.sbuf_tensor(f"rwTh{b}", [128, CH * E], BF16))
            rwTl = p3.enter_context(nc.sbuf_tensor(f"rwTl{b}", [128, CH * E], BF16))
            ylo = [p3.enter_context(nc.sbuf_tensor(f"ylo{b}_{i}", [128, 128], BF16)) for i in range(2)]
            rsc = p3.enter_context(nc.sbuf_tensor(f"rsc{b}", [128, 9 * E + 2], F32))
            coefa = p3.enter_context(nc.sbuf_tensor(f"coefa{b}", [128, NTT * E], F32))
            m8b = p3.enter_context(nc.sbuf_tensor(f"m8b{b}", [128, 16], F32))
            prt = p3.enter_context(nc.psum_tensor(f"prt{b}", [128, 512], F32))
            prl = p3.enter_context(nc.psum_tensor(f"prl{b}", [128, 512], F32))
            pct = p3.enter_context(nc.psum_tensor(f"pct{b}", [128, 1024], F32))

            drw = S.dma("sync", rws[:], dram["router_w"], wait=lt(CONST, BAR_F), inc="dx")
            rtr_toks = []
            prt_free = None
            for cg in range(0, CH, 8):
                n = min(8, CH - cg)
                def rtr(e, cg=cg, n=n):
                    last = None
                    for i_ in range(n):
                        last = e.transpose(prt[:, E * i_:E * i_ + E],
                                           rws[0:E, 128 * (cg + i_):128 * (cg + i_) + 128],
                                           identf[0:E, 0:E])
                    return last
                tR = S.step("tensor", rtr, wait=lt([drw], CONST, prt_free), inc="t")
                evh = S.step("scalar",
                             lambda e, cg=cg, n=n: e.copy(rwTh[:, E * cg:E * cg + E * n],
                                                          prt[:, 0:E * n]),
                             wait=[tR], inc="s")
                evl = S.step("vector",
                             lambda e, cg=cg, n=n: e.tensor_tensor(
                                 rwTl[:, E * cg:E * cg + E * n], prt[:, 0:E * n],
                                 rwTh[:, E * cg:E * cg + E * n], op=OP.subtract),
                             wait=[evh], inc="v")
                prt_free = evl
                rtr_toks.append(evl)
            RWT = lt(*rtr_toks)
            ylo_free = [None, None]
            ct_toks = []
            rsc_free = None
            prl_free = None
            for tt in range(NTT):
                yl_toks = []
                mm_toks = []
                for c in range(CH):
                    sl = c % 2
                    yl = S.step("vector",
                                lambda e, c=c, tt=tt, sl=sl: e.tensor_tensor(
                                    ylo[sl][:], y32_sl(c, 128 * tt, 128), ybf_sl(c, 128 * tt, 128),
                                    op=OP.subtract),
                                wait=lt(Y_READY, ylo_free[sl]), inc="v")
                    def lmm(e, c=c, tt=tt, sl=sl):
                        e.matmul(prl[:, 0:E], ybf_sl(c, 128 * tt, 128), rwTh[:, E * c:E * c + E],
                                 start=(c == 0), stop=False)
                        e.matmul(prl[:, 0:E], ybf_sl(c, 128 * tt, 128), rwTl[:, E * c:E * c + E],
                                 start=False, stop=False)
                        return e.matmul(prl[:, 0:E], ylo[sl][:], rwTh[:, E * c:E * c + E],
                                        start=False, stop=(c == CH - 1))
                    tl = S.step("tensor", lmm,
                                wait=lt([yl], RWT, prl_free if c == 0 else None), inc="t")
                    ylo_free[sl] = tl
                    mm_toks.append(tl)
                sg1 = S.step("scalar",
                             lambda e: e.activation(rsc[:, 0:E], prl[:, 0:E], AF.Exp, scale=-1.0),
                             wait=lt(lt(*mm_toks), rsc_free), inc="s")
                prl_free = sg1
                r1 = S.step("vector", lambda e: e.tensor_scalar(
                    rsc[:, E:2 * E], rsc[:, 0:E], 1.0, None, op0=OP.add), wait=[sg1], inc="v")
                r2 = S.step("vector", lambda e: e.reciprocal(
                    rsc[:, 0:E], rsc[:, E:2 * E]), wait=[r1], inc="v")
                r3 = S.step("vector", lambda e: e.tensor_tensor(
                    rsc[:, E:2 * E], rsc[:, 0:E], ebr[:, 0:E], op=OP.add), wait=lt([r2], CONST), inc="v")
                r4 = S.step("vector", lambda e: e.tensor_reduce(
                    rsc[:, 2 * E:2 * E + EG],
                    rsc[:, E:2 * E].rearrange("p (g k) -> p g k", g=EG),
                    axis=AX.X, op=OP.max), wait=[r3], inc="v")
                r5 = S.step("vector", lambda e: e.match_replace(
                    out=rsc[:, 3 * E:4 * E], in_to_replace=rsc[:, 2 * E:2 * E + EG],
                    in_values=rsc[:, E:2 * E], imm_value=-1e30), wait=[r4], inc="v")
                r6 = S.step("vector", lambda e: e.tensor_reduce(
                    rsc[:, 2 * E + EG:2 * E + 2 * EG],
                    rsc[:, 3 * E:4 * E].rearrange("p (g k) -> p g k", g=EG),
                    axis=AX.X, op=OP.max), wait=[r5], inc="v")
                r7 = S.step("vector", lambda e: e.tensor_tensor(
                    rsc[:, 2 * E:2 * E + EG], rsc[:, 2 * E:2 * E + EG],
                    rsc[:, 2 * E + EG:2 * E + 2 * EG], op=OP.add), wait=[r6], inc="v")
                r8 = S.step("vector", lambda e: e.max(out=m8b[:, 0:8],
                                                      in_=rsc[:, 2 * E:2 * E + EG]),
                            wait=[r7], inc="v")
                r9 = S.step("vector", lambda e: e.tensor_scalar(
                    rsc[:, 2 * E + EG:2 * E + 2 * EG], rsc[:, 2 * E:2 * E + EG],
                    m8b[:, 3:4], None, op0=OP.is_ge), wait=[r8], inc="v")
                r10 = S.step("vector", lambda e: e.tensor_copy(
                    rsc[:, 3 * E:4 * E].rearrange("p (g k) -> p g k", g=EG),
                    rsc[:, 2 * E + EG:2 * E + 2 * EG].rearrange("p (g k) -> p g k", k=1)
                    .to_broadcast([128, EG, EPG])), wait=[r9], inc="v")
                r11 = S.step("vector", lambda e: e.tensor_tensor(
                    rsc[:, 4 * E:5 * E], rsc[:, E:2 * E], rsc[:, 3 * E:4 * E], op=OP.mult),
                    wait=[r10], inc="v")
                r12 = S.step("vector", lambda e: e.tensor_scalar(
                    rsc[:, 5 * E:6 * E], rsc[:, 3 * E:4 * E], -1.0, 1.0,
                    op0=OP.mult, op1=OP.add), wait=[r10], inc="v")
                r13 = S.step("vector", lambda e: e.tensor_tensor(
                    rsc[:, 5 * E:6 * E], rsc[:, 5 * E:6 * E], nib[:, 0:E], op=OP.mult),
                    wait=[r12], inc="v")
                r14 = S.step("vector", lambda e: e.tensor_tensor(
                    rsc[:, 4 * E:5 * E], rsc[:, 4 * E:5 * E], rsc[:, 5 * E:6 * E], op=OP.add),
                    wait=lt(r11, r13), inc="v")
                r15 = S.step("vector", lambda e: e.max(out=m8b[:, 8:16],
                                                       in_=rsc[:, 4 * E:5 * E]),
                             wait=[r14], inc="v")
                r16 = S.step("vector", lambda e: e.tensor_scalar(
                    rsc[:, 6 * E:7 * E], rsc[:, 4 * E:5 * E], m8b[:, 15:16], None,
                    op0=OP.is_ge), wait=[r15], inc="v")
                r17 = S.step("vector", lambda e: e.tensor_tensor(
                    rsc[:, 7 * E:8 * E], rsc[:, 0:E], rsc[:, 6 * E:7 * E], op=OP.mult),
                    wait=[r16], inc="v")
                r18 = S.step("vector", lambda e: e.tensor_reduce(
                    rsc[:, 8 * E:8 * E + 1], rsc[:, 7 * E:8 * E], axis=AX.X, op=OP.add),
                    wait=[r17], inc="v")
                r19 = S.step("vector", lambda e: e.tensor_scalar(
                    rsc[:, 8 * E:8 * E + 1], rsc[:, 8 * E:8 * E + 1], 1e-20, None, op0=OP.add),
                    wait=[r18], inc="v")
                r20 = S.step("vector", lambda e: e.reciprocal(
                    rsc[:, 8 * E + 1:8 * E + 2], rsc[:, 8 * E:8 * E + 1]), wait=[r19], inc="v")
                r21 = S.step("vector", lambda e: e.tensor_scalar(
                    rsc[:, 8 * E + 1:8 * E + 2], rsc[:, 8 * E + 1:8 * E + 2],
                    float(cfg["SCALE_MOE"]), None, op0=OP.mult), wait=[r20], inc="v")
                r22 = S.step("vector", lambda e, tt=tt: e.tensor_scalar(
                    coefa[:, E * tt:E * tt + E], rsc[:, 7 * E:8 * E],
                    rsc[:, 8 * E + 1:8 * E + 2], None, op0=OP.mult), wait=[r21], inc="v")
                rsc_free = r22
                ct_toks.append(r22)
            pct_free = None
            for t0 in range(0, NTT, 8):
                n = min(8, NTT - t0)
                def ctr(e, t0=t0, n=n):
                    last = None
                    for i_ in range(n):
                        last = e.transpose(pct[0:E, 128 * i_:128 * i_ + 128],
                                           coefa[:, E * (t0 + i_):E * (t0 + i_) + E],
                                           identf[:])
                    return last
                tC = S.step("tensor", ctr, wait=lt(*ct_toks[t0:t0 + n], pct_free), inc="t")
                evC = S.step("scalar",
                             lambda e, t0=t0, n=n: e.copy(coefT[0:E, 128 * t0:128 * t0 + 128 * n],
                                                          pct[0:E, 0:128 * n]),
                             wait=lt([tC], [ct_init]), inc="s")
                pct_free = evC
            COEF = lt(("s", S.counts["s"]))

        pd_ = [s_f.enter_context(nc.psum_tensor(f"pd{b}_{i}", [128, 512], F32)) for i in range(2)]
        pcb = s_f.enter_context(nc.psum_tensor(f"pcbx{b}", [128, 512], F32))

        st = dict(pgu_free=[None] * 4, pd_free=[None, None], sg_free=[None, None],
                  tm1_free=[None, None], cb_free=[None, None], pcb_free=None,
                  slotB=[[], []], slotW=[[], []], pendB=[])

        def flushB(kmax):
            n = 0
            while st["pendB"] and n < kmax:
                st["pendB"].pop(0)()
                n += 1

        for e_i in range(NEXP):
            par = e_i % 2
            is_sh = e_i >= E

            for tg in range(NTG):
                if tg == 1 and 1 <= e_i and e_i + 1 < NEXP:
                    parn = (e_i + 1) % 2
                    wfree = lt(*st["slotW"][parn]) if st["slotW"][parn] else None
                    st["slotW"][parn] = []
                    emit_wdma(e_i + 1, wait=wfree)
                hts = tg % 2
                csl = tg % 2
                flushB(10 ** 9)
                hslot_free = lt(*st["slotB"][hts]) if st["slotB"][hts] else None
                st["slotB"][hts] = []
                if not is_sh:
                    def cbm(e, e_i=e_i, tg=tg):
                        b0 = 32 * (e_i // 32)
                        return e.matmul(pcb[:, 0:TG],
                                        selb[b0:b0 + 32, 128 * (e_i % 32):128 * (e_i % 32) + 128],
                                        coefT[b0:b0 + 32, TG * tg:TG * tg + TG],
                                        start=True, stop=True,
                                        tile_position=(b0, 0))
                    tcb = S.step("tensor", cbm, wait=lt(SEL, COEF, st["pcb_free"]), inc="t")
                    ecb = S.step("scalar",
                                 lambda e, csl=csl: e.copy(coefb[csl][:], pcb[:, 0:TG]),
                                 wait=lt([tcb], st["cb_free"][csl]), inc="s")
                    st["pcb_free"] = ecb
                else:
                    ecb = None

                h_toks = []
                for ic in range(IC):
                    bp = ic % 2
                    bg, bu = 2 * bp, 2 * bp + 1
                    def gmm(e, par=par, ic=ic, tg=tg, bg=bg):
                        last = None
                        for c in range(CH):
                            last = e.matmul(pgu[bg][:, 0:TG],
                                            wg[par][:, c * I + 128 * ic: c * I + 128 * ic + 128],
                                            ybf_sl(c, TG * tg, TG),
                                            start=(c == 0), stop=(c == CH - 1))
                        return last
                    tgk = S.step("tensor", gmm,
                                 wait=lt(wload_g[par], Y_READY, st["pgu_free"][bg]), inc="t")
                    def umm(e, par=par, ic=ic, tg=tg, bu=bu):
                        last = None
                        for c in range(CH):
                            last = e.matmul(pgu[bu][:, 0:TG],
                                            wu[par][:, c * I + 128 * ic: c * I + 128 * ic + 128],
                                            ybf_sl(c, TG * tg, TG),
                                            start=(c == 0), stop=(c == CH - 1))
                        return last
                    tuk = S.step("tensor", umm, wait=lt(st["pgu_free"][bu], wload_u[par]), inc="t")
                    flushB(2)
                    ssl = ic % 2
                    stk = S.step("scalar",
                                 lambda e, bg=bg, ssl=ssl: e.activation(
                                     sg_[ssl][:], pgu[bg][:, 0:TG], AF.Silu),
                                 wait=lt([tgk], st["sg_free"][ssl]), inc="s")
                    if not is_sh:
                        v1 = S.step("vector",
                                    lambda e, ssl=ssl, bu=bu: e.tensor_tensor(
                                        tm1[ssl][:], sg_[ssl][:], pgu[bu][:, 0:TG], op=OP.mult),
                                    wait=lt([stk], [tuk], st["tm1_free"][ssl]), inc="v")
                        pp = S.step("gpsimd",
                                    lambda e, ssl=ssl, csl=csl, hts=hts, ic=ic: e.tensor_tensor(
                                        hT[:, (hts * IC + ic) * TG:(hts * IC + ic) * TG + TG],
                                        tm1[ssl][:], coefb[csl][:], op=OP.mult),
                                    wait=lt([v1], [ecb], hslot_free), inc="p")
                        st["tm1_free"][ssl] = pp
                        h_toks.append(pp)
                    else:
                        v1 = S.step("vector",
                                    lambda e, ssl=ssl, bu=bu, hts=hts, ic=ic: e.tensor_tensor(
                                        hT[:, (hts * IC + ic) * TG:(hts * IC + ic) * TG + TG],
                                        sg_[ssl][:], pgu[bu][:, 0:TG], op=OP.mult),
                                    wait=lt([stk], [tuk], hslot_free), inc="v")
                        pp = v1
                        h_toks.append(pp)
                    st["pgu_free"][bg] = stk
                    st["pgu_free"][bu] = v1
                    st["sg_free"][ssl] = v1
                if not is_sh:
                    st["cb_free"][csl] = h_toks[-1]
                HT = lt(*h_toks)

                def mk_thunk(hout, bank, par=par, tg=tg, hts=hts, HT=HT):
                    def f():
                        def dmm(e, par=par, tg=tg, hts=hts, hout=hout, bank=bank):
                            last = None
                            for ic in range(IC):
                                last = e.matmul(
                                    pd_[bank][:, 0:TG],
                                    wd[par][:, ic * C + 128 * hout: ic * C + 128 * hout + 128],
                                    hT[:, (hts * IC + ic) * TG:(hts * IC + ic) * TG + TG],
                                    start=(ic == 0), stop=(ic == IC - 1))
                            return last
                        td = S.step("tensor", dmm,
                                    wait=lt(HT, st["pd_free"][bank], wload_d[par]), inc="t")
                        ev = S.step("vector",
                                    lambda e, hout=hout, tg=tg, bank=bank: e.tensor_tensor(
                                        y32_sl(hout, TG * tg, TG), y32_sl(hout, TG * tg, TG),
                                        pd_[bank][:, 0:TG], op=OP.add),
                                    wait=[td], inc="v")
                        st["pd_free"][bank] = ev
                        st["slotB"][hts].append(ev)
                        st["slotW"][par].append(td)
                    return f
                for hout in range(CH):
                    st["pendB"].append(mk_thunk(hout, hout % 2))
        flushB(10 ** 9)
        YACC = S.barrier()
        s_f.close()

        with ExitStack() as p5:
            lnb2 = alloc_ln_bufs(nc, p5, b, 1, TG)
            pstA2 = p5.enter_context(nc.psum_tensor(f"pstA2{b}", [128, 512], F32))
            pstB2 = p5.enter_context(nc.psum_tensor(f"pstB2{b}", [128, 512], F32))
            pbc2 = p5.enter_context(nc.psum_tensor(f"pbc2{b}", [128, 1024], F32))
            pot = [p5.enter_context(nc.psum_tensor(f"pot{b}_{i}", [128, 1024], F32)) for i in range(2)]
            ostg = [p5.enter_context(nc.sbuf_tensor(f"ostg{b}_{i}", [128, C], F32)) for i in range(2)]
            ln_state = dict(prev=None)
            ln_toks = []
            for tg in range(NTG):
                ln_last = emit_ln(nc, S, cfg, tg,
                                  lambda c, tg=tg: y32[:, c * T + TG * tg: c * T + TG * tg + TG],
                                  y32_sl, ybf_sl, lnb2, pstA2, pstB2, pbc2, olnb, ob32,
                                  [YACC], ln_state)
                ln_toks.append(ln_last)
            LN2 = lt(*ln_toks)
            ostg_free = [None, None]
            pot_free = [None, None]
            for tt in range(NTT):
                sl = tt % 2
                osn = f"o{sl}"
                def otr(e, tt=tt, sl=sl):
                    last = None
                    for c in range(CH):
                        last = e.transpose(pot[sl][:, 128 * c:128 * c + 128],
                                           y32_sl(c, 128 * tt, 128), identf[:])
                    return last
                tO = S.step("tensor", otr, wait=lt(LN2, pot_free[sl]), inc="t")
                evO = S.step("vector",
                             lambda e, sl=sl: e.tensor_copy(ostg[sl][:], pot[sl][:, 0:C]),
                             wait=lt([tO], ostg_free[sl]), inc="v")
                pot_free[sl] = evO
                dO = S.dma("sync", y_out[b, 128 * tt:128 * tt + 128, :], ostg[sl][:],
                           wait=[evO], inc=osn)
                ostg_free[sl] = dO


def alloc_ln_bufs(nc, stack, b, idx, TG):
    bufs = {}
    bufs["xab"] = [stack.enter_context(nc.sbuf_tensor(f"ln{idx}xab{b}_{i}", [128, TG], BF16)) for i in range(2)]
    bufs["sqb"] = [stack.enter_context(nc.sbuf_tensor(f"ln{idx}sqb{b}_{i}", [128, TG], BF16)) for i in range(2)]
    bufs["rows"] = stack.enter_context(nc.sbuf_tensor(f"ln{idx}rows{b}", [32, 6 * TG], F32))
    bufs["rbh"] = stack.enter_context(nc.sbuf_tensor(f"ln{idx}rbh{b}", [32, 4 * TG], BF16))
    bufs["t1"] = stack.enter_context(nc.sbuf_tensor(f"ln{idx}t1{b}", [128, TG], F32))
    return bufs


def emit_ln(nc, S, cfg, tg, src_sl_fn, y_sl, ybf_sl, bufs, pstA, pstB, pbc,
            olnb, ob32, in_toks, state):
    CH, T, TG, EPS = cfg["CH"], cfg["T"], cfg["TG"], cfg["EPS"]
    xab, sqb, rows, rbh, t1 = bufs["xab"], bufs["sqb"], bufs["rows"], bufs["rbh"], bufs["t1"]
    prev = state.get("prev")
    xab_free = state.get("xab_free", [None, None])
    sqb_free = state.get("sqb_free", [None, None])
    IN = lt(*in_toks)
    stat_toks = []
    cast_toks = []
    for c in range(CH):
        sl = c % 2
        ct = S.step("vector",
                    lambda e, c=c, sl=sl: e.tensor_copy(xab[sl][:], src_sl_fn(c)),
                    wait=lt(IN, xab_free[sl], prev), inc="v")
        sq = S.step("gpsimd",
                    lambda e, sl=sl: e.tensor_tensor(sqb[sl][:], xab[sl][:], xab[sl][:], op=OP.mult),
                    wait=lt([ct], sqb_free[sl]), inc="p")
        m1 = S.step("tensor",
                    lambda e, sl=sl, c=c: e.matmul(pstA[0:32, 0:TG], olnb[:], xab[sl][:],
                                                   start=(c == 0), stop=(c == CH - 1)),
                    wait=lt([ct], state.get("pst_free") if c == 0 else None), inc="t")
        m2 = S.step("tensor",
                    lambda e, sl=sl, c=c: e.matmul(pstB[0:32, 0:TG], olnb[:], sqb[sl][:],
                                                   start=(c == 0), stop=(c == CH - 1)),
                    wait=[sq], inc="t")
        xab_free[sl] = lt(m1, sq)
        sqb_free[sl] = m2
        stat_toks.append(lt(m1, m2))
        cast_toks.append(ct)
    STAT = lt(*stat_toks)
    r1 = S.step("vector", lambda e: (
        e.tensor_copy(rows[:, 0:TG], pstA[0:32, 0:TG]),
        e.tensor_copy(rows[:, TG:2 * TG], pstB[0:32, 0:TG]))[-1],
        wait=lt(STAT, state.get("rows_free")), inc="v")
    state["pst_free"] = r1
    r2 = S.step("vector", lambda e: e.tensor_tensor(
        rows[:, 2 * TG:3 * TG], rows[:, 0:TG], rows[:, 0:TG], op=OP.mult), wait=[r1], inc="v")
    r3 = S.step("vector", lambda e: e.tensor_tensor(
        rows[:, 2 * TG:3 * TG], rows[:, TG:2 * TG], rows[:, 2 * TG:3 * TG], op=OP.subtract),
        wait=[r2], inc="v")
    r4 = S.step("vector", lambda e: e.tensor_scalar(
        rows[:, 2 * TG:3 * TG], rows[:, 2 * TG:3 * TG], float(EPS), None, op0=OP.add),
        wait=[r3], inc="v")
    r5a = S.step("scalar", lambda e: e.activation(
        rows[:, 3 * TG:4 * TG], rows[:, 2 * TG:3 * TG], AF.Ln), wait=[r4], inc="s")
    r5 = S.step("scalar", lambda e: e.activation(
        rows[:, 4 * TG:5 * TG], rows[:, 3 * TG:4 * TG], AF.Exp, scale=-0.5),
        wait=[r5a], inc="s")
    r6 = S.step("vector", lambda e: e.tensor_tensor(
        rows[:, 5 * TG:6 * TG], rows[:, 0:TG], rows[:, 4 * TG:5 * TG], op=OP.mult),
        wait=[r5], inc="v")
    r7 = S.step("vector", lambda e: (
        e.tensor_copy(rbh[:, 0:TG], rows[:, 4 * TG:5 * TG]),
        e.tensor_copy(rbh[:, 2 * TG:3 * TG], rows[:, 5 * TG:6 * TG]))[-1],
        wait=[r6], inc="v")
    r8 = S.step("vector", lambda e: (
        e.tensor_tensor(rbh[:, TG:2 * TG], rows[:, 4 * TG:5 * TG], rbh[:, 0:TG], op=OP.subtract),
        e.tensor_tensor(rbh[:, 3 * TG:4 * TG], rows[:, 5 * TG:6 * TG], rbh[:, 2 * TG:3 * TG],
                        op=OP.subtract))[-1],
        wait=[r7], inc="v")
    state["rows_free"] = r8
    bc1 = S.step("tensor", lambda e: (
        e.matmul(pbc[:, 0:TG], ob32[:], rbh[:, 0:TG], start=True, stop=False),
        e.matmul(pbc[:, 0:TG], ob32[:], rbh[:, TG:2 * TG], start=False, stop=True))[-1],
        wait=lt([r8], state.get("pbc_free")), inc="t")
    bc2 = S.step("tensor", lambda e: (
        e.matmul(pbc[:, 512:512 + TG], ob32[:], rbh[:, 2 * TG:3 * TG], start=True, stop=False),
        e.matmul(pbc[:, 512:512 + TG], ob32[:], rbh[:, 3 * TG:4 * TG], start=False, stop=True))[-1],
        wait=[bc1], inc="t")
    BC = lt(bc1, bc2)
    outs = []
    t1_free = None
    for c in range(CH):
        o1 = S.step("vector",
                    lambda e, c=c: e.tensor_tensor(t1[:], src_sl_fn(c), pbc[:, 0:TG], op=OP.mult),
                    wait=lt(BC, t1_free, cast_toks[c]), inc="v")
        o2 = S.step("vector",
                    lambda e, c=c, tg=tg: e.tensor_tensor(
                        y_sl(c, TG * tg, TG), t1[:], pbc[:, 512:512 + TG], op=OP.subtract),
                    wait=[o1], inc="v")
        o3 = S.step("vector",
                    lambda e, c=c, tg=tg: e.tensor_copy(ybf_sl(c, TG * tg, TG), y_sl(c, TG * tg, TG)),
                    wait=[o2], inc="v")
        t1_free = o2
        outs.append(o3)
    last = lt(*outs)
    state["prev"] = last
    state["pbc_free"] = last
    state["xab_free"] = xab_free
    state["sqb_free"] = sqb_free
    return last


def kernel(L0_attn_x, L0_attn_W_attn, L0_attn_W_proj, L0_attn_lambda_q1,
           L0_attn_lambda_k1, L0_attn_lambda_q2, L0_attn_lambda_k2,
           L0_ffn_router_w, L0_ffn_e_bias, L0_ffn_Wg, L0_ffn_Wu, L0_ffn_Wd,
           L0_ffn_SWg, L0_ffn_SWu, L0_ffn_SWd):
    cfg = make_cfg(full=True)
    con = host_constants(cfg)
    wattn_p = permute_wattn(cfg, L0_attn_W_attn)
    lam_rep = np.concatenate([
        np.tile(np.asarray(L0_attn_lambda_q1, dtype=np.float32)[None, :], (128, 1)),
        np.tile(np.asarray(L0_attn_lambda_k1, dtype=np.float32)[None, :], (128, 1)),
        np.tile(np.asarray(L0_attn_lambda_q2, dtype=np.float32)[None, :], (128, 1)),
        np.tile(np.asarray(L0_attn_lambda_k2, dtype=np.float32)[None, :], (128, 1))], axis=1)
    eb_rep = np.tile(np.asarray(L0_ffn_e_bias, dtype=np.float32)[None, :], (128, 1))

    nc = build_program(cfg)
    x = np.ascontiguousarray(np.asarray(L0_attn_x, dtype=np.float32))
    in_maps = []
    for core in range(N_CORES):
        m = dict(x=np.ascontiguousarray(x[2 * core:2 * core + 2]),
                 wattn=wattn_p.astype(np.float32),
                 wproj=np.ascontiguousarray(np.asarray(L0_attn_W_proj, dtype=np.float32)),
                 router_w=np.ascontiguousarray(np.asarray(L0_ffn_router_w, dtype=np.float32)),
                 Wg=np.ascontiguousarray(np.asarray(L0_ffn_Wg, dtype=np.float32)),
                 Wu=np.ascontiguousarray(np.asarray(L0_ffn_Wu, dtype=np.float32)),
                 Wd=np.ascontiguousarray(np.asarray(L0_ffn_Wd, dtype=np.float32)),
                 SWg=np.ascontiguousarray(np.asarray(L0_ffn_SWg, dtype=np.float32)),
                 SWu=np.ascontiguousarray(np.asarray(L0_ffn_SWu, dtype=np.float32)),
                 SWd=np.ascontiguousarray(np.asarray(L0_ffn_SWd, dtype=np.float32)),
                 lam_rep=lam_rep, ebias_rep=eb_rep)
        for k, v in con.items():
            m["c_" + k] = v.astype(np.float32)
        in_maps.append(m)
    res = run_bass_kernel_spmd(nc, in_maps, core_ids=list(range(N_CORES)))
    out = np.concatenate([np.asarray(r["y"]) for r in res.results], axis=0)
    return out.astype(np.float32)


# version 11
# speedup vs baseline: 1.1090x; 1.1090x over previous
import math
import numpy as np
from contextlib import ExitStack

import concourse.bass as bass
import concourse.mybir as mybir
from concourse.alu_op_type import AluOpType
from concourse.bass_utils import run_bass_kernel_spmd

F32 = mybir.dt.float32
BF16 = mybir.dt.bfloat16
AF = mybir.ActivationFunctionType
AX = mybir.AxisListType
OP = AluOpType

N_CORES = 8


class Sched:
    ENGINES = ["tensor", "vector", "scalar", "gpsimd", "sync"]

    def __init__(self):
        self.steps = []
        self.counts = {}

    def step(self, engine, emitter, wait=(), inc=None, amt=1, tag=""):
        tok = None
        if inc is not None:
            self.counts[inc] = self.counts.get(inc, 0) + amt
            tok = (inc, self.counts[inc])
        wait = [w for w in wait if w is not None]
        self.steps.append((engine, emitter, tuple(wait), inc, amt, tag))
        return tok

    def barrier(self):
        return [(k, v) for k, v in self.counts.items() if v > 0]

    def dma(self, queue, out, in_, wait=(), inc=None, tag=""):
        return self.step(queue, lambda e, o=out, i=in_: e.dma_start(out=o, in_=i),
                         wait=wait, inc=inc, amt=16, tag=tag)

    def emit(self, block, sems):
        for eng_name in self.ENGINES:
            mysteps = [st for st in self.steps if st[0] == eng_name]
            if not mysteps:
                continue

            def make(fn_steps):
                def _(eng):
                    seen = {}
                    for (_, emitter, waits, inc, amt, tag) in fn_steps:
                        for (sem, val) in waits:
                            if val <= 0 or seen.get(sem, 0) >= val:
                                continue
                            eng.wait_ge(sems[sem], val)
                            seen[sem] = val
                        instr = emitter(eng)
                        if inc is not None:
                            if instr is None:
                                eng.sem_inc(sems[inc], amt)
                            else:
                                instr.then_inc(sems[inc], amt)
                return _

            getattr(block, eng_name)(make(mysteps))


def lt(*toks):
    best = {}
    for t in toks:
        if t is None:
            continue
        if isinstance(t, (list, tuple)) and t and isinstance(t[0], (list, tuple)):
            for u in t:
                if u is not None:
                    best[u[0]] = max(best.get(u[0], 0), u[1])
            continue
        if isinstance(t, list):
            for u in t:
                if u is not None:
                    best[u[0]] = max(best.get(u[0], 0), u[1])
            continue
        best[t[0]] = max(best.get(t[0], 0), t[1])
    return [(s, v) for s, v in best.items()]


def make_cfg(full=True):
    if full:
        c = dict(B=2, T=2048, C=1024, NH=16, D=32, DV=64, E=64, EG=8, I=512, ISH=1024)
    else:
        import os as _os
        _rt = int(_os.environ.get('RED_T', '512'))
        _ri = int(_os.environ.get('RED_I', '128'))
        c = dict(B=1, T=_rt, C=1024, NH=16, D=32, DV=64, E=16, EG=8, I=_ri, ISH=2*_ri)
    c["NH2"] = 2 * c["NH"]
    c["CH"] = c["C"] // 128
    c["NTT"] = c["T"] // 128
    c["TG"] = min(512, c["T"])
    c["NTG"] = c["T"] // c["TG"]
    c["QPG"] = c["TG"] // 128
    c["IC"] = c["I"] // 128
    c["ICS"] = c["ISH"] // 128
    c["NSQ"] = max(1, c["NH2"] // 16)
    c["HPSQ"] = c["NH"] // c["NSQ"]
    c["EPG"] = c["E"] // c["EG"]
    c["LAM_INIT"] = 0.2
    c["EPS"] = 1e-5
    c["SCALE_MOE"] = 2.5
    c["ROPE_BASE"] = 10000.0
    c["SM_SCALE"] = 1.0 / math.sqrt(c["D"])
    return c


def host_constants(cfg):
    T, C, D, NH2, E = cfg["T"], cfg["C"], cfg["D"], cfg["NH2"], cfg["E"]
    half = D // 2
    con = {}
    con["identf"] = np.eye(128, dtype=np.float32)
    con["identb"] = np.eye(128, dtype=np.float32)
    theta = cfg["ROPE_BASE"] ** (-(np.arange(half, dtype=np.float64) * 2.0 / D))
    ang = np.arange(T, dtype=np.float64)[None, :] * theta[:, None]
    rows = np.arange(128) % half
    con["cosr"] = np.cos(ang)[rows, :].astype(np.float32)
    con["sinr"] = np.sin(ang)[rows, :].astype(np.float32)
    nch = NH2 * D // 128
    P1 = np.zeros((nch, 128, 128), dtype=np.float32)
    P2 = np.zeros((nch, 128, 128), dtype=np.float32)
    for h in range(NH2):
        dst_c, dst_r0 = h // 4, 32 * (h % 4)
        src_r0 = 16 * (h % 8)
        for j in range(half):
            P1[dst_c, src_r0 + j, dst_r0 + j] = 1.0
            P2[dst_c, src_r0 + j, dst_r0 + 16 + j] = 1.0
    con["permx1"] = np.ascontiguousarray(P1.transpose(1, 0, 2).reshape(128, nch * 128))
    con["permx2"] = np.ascontiguousarray(P2.transpose(1, 0, 2).reshape(128, nch * 128))
    QPG, TG = cfg["QPG"], cfg["TG"]
    cm = np.zeros((QPG, 128, TG), dtype=np.float32)
    for v in range(QPG):
        q = v * 128 + np.arange(128)
        k = np.arange(TG)
        cm[v] = (k[None, :] <= q[:, None]).astype(np.float32)
    con["cmask"] = np.ascontiguousarray(cm.transpose(1, 0, 2).reshape(128, QPG * TG))
    con["neg_iota"] = np.tile(-np.arange(E, dtype=np.float32) * 1e-20, (128, 1))
    con["ones_ln"] = np.full((128, 32), 1.0 / C, dtype=np.float32)
    con["ones_b32"] = np.full((32, 128), 1.0 / 32, dtype=np.float32)
    sel = np.zeros((64, 32 * 128), dtype=np.float32)
    for j in range(32):
        sel[j, 128 * j:128 * j + 128] = 1.0
        sel[32 + j, 128 * j:128 * j + 128] = 1.0
    con["sel_bank"] = sel
    return con


def permute_wattn(cfg, W):
    C, NH2, D = cfg["C"], cfg["NH2"], cfg["D"]
    half = D // 2
    perm = np.zeros(C, dtype=np.int64)
    for h in range(NH2):
        for j in range(half):
            perm[128 * (h // 8) + 16 * (h % 8) + j] = h * D + 2 * j
            perm[C // 2 + 128 * (h // 8) + 16 * (h % 8) + j] = h * D + 2 * j + 1
    Wq = np.asarray(W)[:, 0:C][:, perm]
    Wk = np.asarray(W)[:, C:2 * C][:, perm]
    Wv = np.asarray(W)[:, 2 * C:3 * C]
    return np.ascontiguousarray(np.concatenate([Wq, Wk, Wv], axis=1))


def build_program(cfg):
    nc = bass.Bass("TRN2", target_bir_lowering=False)
    B, T, C, CH = cfg["B"], cfg["T"], cfg["C"], cfg["CH"]
    NH, NH2, D, DV = cfg["NH"], cfg["NH2"], cfg["D"], cfg["DV"]
    E, I, ISH, EG, EPG = cfg["E"], cfg["I"], cfg["ISH"], cfg["EG"], cfg["EPG"]
    IC, ICS = cfg["IC"], cfg["ICS"]
    TG, NTG, NTT, QPG = cfg["TG"], cfg["NTG"], cfg["NTT"], cfg["QPG"]
    NSQ, HPSQ = cfg["NSQ"], cfg["HPSQ"]
    SMS = cfg["SM_SCALE"]
    EPS = cfg["EPS"]
    nch = NH2 * D // 128
    ICH = ICS // 2

    dram = {}
    dram["x"] = nc.dram_tensor("x", [B, T, C], F32, kind="ExternalInput").ap()
    dram["wattn"] = nc.dram_tensor("wattn", [C, 2 * nch * 128 + NH * DV], F32, kind="ExternalInput").ap()
    dram["wproj"] = nc.dram_tensor("wproj", [C, C], F32, kind="ExternalInput").ap()
    dram["router_w"] = nc.dram_tensor("router_w", [E, C], F32, kind="ExternalInput").ap()
    dram["Wg"] = nc.dram_tensor("Wg", [E, C, I], F32, kind="ExternalInput").ap()
    dram["Wu"] = nc.dram_tensor("Wu", [E, C, I], F32, kind="ExternalInput").ap()
    dram["Wd"] = nc.dram_tensor("Wd", [E, I, C], F32, kind="ExternalInput").ap()
    dram["SWg"] = nc.dram_tensor("SWg", [C, ISH], F32, kind="ExternalInput").ap()
    dram["SWu"] = nc.dram_tensor("SWu", [C, ISH], F32, kind="ExternalInput").ap()
    dram["SWd"] = nc.dram_tensor("SWd", [ISH, C], F32, kind="ExternalInput").ap()
    dram["lam_rep"] = nc.dram_tensor("lam_rep", [128, 4 * D], F32, kind="ExternalInput").ap()
    dram["ebias_rep"] = nc.dram_tensor("ebias_rep", [128, E], F32, kind="ExternalInput").ap()
    for name, shape in [("identf", [128, 128]), ("identb", [128, 128]),
                        ("cosr", [128, T]), ("sinr", [128, T]),
                        ("permx1", [128, nch * 128]), ("permx2", [128, nch * 128]),
                        ("cmask", [128, QPG * TG]), ("neg_iota", [128, E]),
                        ("ones_ln", [128, 32]), ("ones_b32", [32, 128]),
                        ("sel_bank", [64, 32 * 128])]:
        dram[name] = nc.dram_tensor("c_" + name, shape, F32, kind="ExternalInput").ap()
    xt_scr = nc.dram_tensor("xt_scratch", [B, C, T], F32, kind="Internal").ap()
    dT_scr = nc.dram_tensor("dT_scratch", [B, C, T], BF16, kind="Internal").ap()
    y_out = nc.dram_tensor("y", [B, T, C], F32, kind="ExternalOutput").ap()

    S = Sched()
    sems = {}
    SEM_NAMES = ["t", "v", "s", "p", "dc", "dcg", "dx", "dx0", "dx1", "dsc0", "dsc1",
                 "dxs", "dwa", "dw0", "dw1", "o0", "o1", "dd0", "dd1", "ddl",
                 "dwg0", "dwg1", "dwu0", "dwu1", "dwd0", "dwd1"]

    with ExitStack() as top:
        identf = top.enter_context(nc.sbuf_tensor("identf", [128, 128], F32))
        identb = top.enter_context(nc.sbuf_tensor("identb", [128, 128], BF16))
        nib = top.enter_context(nc.sbuf_tensor("nib", [128, E], F32))
        olnb = top.enter_context(nc.sbuf_tensor("olnb", [128, 32], BF16))
        ob32 = top.enter_context(nc.sbuf_tensor("ob32", [32, 128], BF16))
        lamr = top.enter_context(nc.sbuf_tensor("lamr", [128, 4 * D], F32))
        ebr = top.enter_context(nc.sbuf_tensor("ebr", [128, E], F32))
        lamv = top.enter_context(nc.sbuf_tensor("lamv", [128, 8], F32))
        epsc = top.enter_context(nc.sbuf_tensor("epsc", [128, 1], F32))

        for s_ in SEM_NAMES:
            sems[s_] = top.enter_context(nc.semaphore(s_))

        S.dma("sync", identf[:], dram["identf"], inc="dc")
        S.dma("gpsimd", identb[:], dram["identb"], inc="dcg")
        S.dma("sync", nib[:], dram["neg_iota"], inc="dc")
        S.dma("gpsimd", olnb[:], dram["ones_ln"], inc="dcg")
        S.dma("gpsimd", ob32[:], dram["ones_b32"], inc="dcg")
        S.dma("sync", lamr[:], dram["lam_rep"], inc="dc")
        S.dma("sync", ebr[:], dram["ebias_rep"], inc="dc")
        CONST = [("dc", S.counts["dc"]), ("dcg", S.counts["dcg"])]

        S.step("vector", lambda e: e.memset(epsc[:], 1e-5), inc="v")
        t1_ = S.step("vector", lambda e: (
            e.tensor_tensor(lamr[:, 0:D], lamr[:, 0:D], lamr[:, D:2 * D], op=OP.mult),
            e.tensor_tensor(lamr[:, 2 * D:3 * D], lamr[:, 2 * D:3 * D], lamr[:, 3 * D:4 * D], op=OP.mult))[-1],
            wait=CONST, inc="v")
        t2_ = S.step("vector", lambda e: (
            e.tensor_reduce(lamv[:, 0:1], lamr[:, 0:D], axis=AX.X, op=OP.add),
            e.tensor_reduce(lamv[:, 1:2], lamr[:, 2 * D:3 * D], axis=AX.X, op=OP.add))[-1],
            wait=[t1_], inc="v")
        t3_ = S.step("scalar", lambda e: (
            e.activation(lamv[:, 2:3], lamv[:, 0:1], AF.Exp),
            e.activation(lamv[:, 3:4], lamv[:, 1:2], AF.Exp))[-1],
            wait=[t2_], inc="s")
        t4_ = S.step("vector", lambda e: e.tensor_tensor(lamv[:, 4:5], lamv[:, 2:3],
                                                         lamv[:, 3:4], op=OP.subtract),
                     wait=[t3_], inc="v")
        t5_ = S.step("vector", lambda e: e.tensor_scalar(
            lamv[:, 5:6], lamv[:, 4:5], float(cfg["LAM_INIT"]), -1.0,
            op0=OP.add, op1=OP.mult), wait=[t4_], inc="v")
        LAM = S.step("vector", lambda e: e.tensor_scalar(
            lamv[:, 4:5], lamv[:, 4:5], float(cfg["LAM_INIT"]), None, op0=OP.add),
            wait=[t5_], inc="v")

        consts = dict(identf=identf, identb=identb, nib=nib, olnb=olnb, ob32=ob32,
                      ebr=ebr, lamv=lamv, epsc=epsc, CONST=CONST, LAM=LAM)

        for b in range(B):
            emit_batch(nc, S, cfg, b, dram, xt_scr, dT_scr, y_out, consts)

        fin = []
        for s_ in ["o0", "o1"]:
            if S.counts.get(s_, 0):
                fin.append((s_, S.counts[s_]))
        S.step("sync", lambda e: e.nop(), wait=fin)
        blk = top.enter_context(nc.Block())
        S.emit(blk, sems)
    return nc


def emit_batch(nc, S, cfg, b, dram, xt_scr, dT_scr, y_out, K):
    B, T, C, CH = cfg["B"], cfg["T"], cfg["C"], cfg["CH"]
    NH, NH2, D, DV = cfg["NH"], cfg["NH2"], cfg["D"], cfg["DV"]
    E, I, ISH, EG, EPG = cfg["E"], cfg["I"], cfg["ISH"], cfg["EG"], cfg["EPG"]
    IC, ICS = cfg["IC"], cfg["ICS"]
    TG, NTG, NTT, QPG = cfg["TG"], cfg["NTG"], cfg["NTT"], cfg["QPG"]
    NSQ, HPSQ = cfg["NSQ"], cfg["HPSQ"]
    SMS, EPS = cfg["SM_SCALE"], cfg["EPS"]
    nch = NH2 * D // 128
    ICH = ICS // 2
    identf, identb = K["identf"], K["identb"]
    nib, olnb, ob32, ebr, lamv = K["nib"], K["olnb"], K["ob32"], K["ebr"], K["lamv"]
    epsc = K["epsc"]
    CONST, LAM = K["CONST"], K["LAM"]

    with ExitStack() as bs:
        s_x = ExitStack()
        xbf = s_x.enter_context(nc.sbuf_tensor(f"xbf{b}", [128, CH * T], BF16))

        def xbf_sl(c, t0, n):
            return xbf[:, c * T + t0: c * T + t0 + n]

        with ExitStack() as p0:
            xt = [p0.enter_context(nc.sbuf_tensor(f"p0xt{b}_{i}", [128, C], F32)) for i in range(2)]
            xstg = [p0.enter_context(nc.sbuf_tensor(f"p0st{b}_{i}", [128, C], F32)) for i in range(2)]
            pt = [p0.enter_context(nc.psum_tensor(f"p0ps{b}_{i}", [128, 512], F32)) for i in range(2)]
            xt_free = [None, None]
            stg_free = [None, None]
            BAR_B = S.barrier()
            for tt in range(NTT):
                sl = tt % 2
                dxn = f"dx{sl}"
                d_tok = S.dma("sync", xt[sl][:], dram["x"][b, 128 * tt:128 * tt + 128, :],
                              wait=lt(xt_free[sl], BAR_B if tt < 2 else None), inc=dxn)
                ev_toks = []
                tr_toks = []
                for half_c in range(2):
                    cg = 4 * half_c
                    psl = pt[half_c]
                    def trs(e, sl=sl, cg=cg, psl=psl):
                        last = None
                        for i_ in range(4):
                            last = e.transpose(psl[:, 128 * i_:128 * i_ + 128],
                                               xt[sl][:, 128 * (cg + i_):128 * (cg + i_) + 128],
                                               identf[:])
                        return last
                    tr = S.step("tensor", trs,
                                wait=lt([d_tok], CONST, ev_toks[-1] if ev_toks else None,
                                        stg_free[sl]), inc="t")
                    ev = S.step("vector",
                                lambda e, sl=sl, cg=cg, psl=psl: e.tensor_copy(
                                    xstg[sl][:, 128 * cg:128 * cg + 512], psl[:, 0:512]),
                                wait=[tr], inc="v")
                    tr_toks.append(tr)
                    ev_toks.append(ev)
                xt_free[sl] = tr_toks[-1]
                EV = lt(*ev_toks)
                sc = S.step("scalar",
                            lambda e, sl=sl, tt=tt: e.copy(
                                xbf[:].rearrange("p (c t) -> p c t", c=CH)[:, :, 128 * tt:128 * tt + 128],
                                xstg[sl][:].rearrange("p (c n) -> p c n", c=CH)),
                            wait=EV, inc="s")
                dsc = S.dma("sync",
                            xt_scr[b].rearrange("(c p) t -> p c t", p=128)[:, :, 128 * tt:128 * tt + 128],
                            xstg[sl][:].rearrange("p (c n) -> p c n", c=CH),
                            wait=EV, inc=f"dsc{sl}")
                stg_free[sl] = lt(sc, dsc)
            XBF_READY = lt(("s", S.counts["s"]))
            XSCR_DONE = lt(("dsc0", S.counts.get("dsc0", 0)), ("dsc1", S.counts.get("dsc1", 0)))

        with ExitStack() as p1:
            cosb = p1.enter_context(nc.sbuf_tensor(f"cosb{b}", [128, T], BF16))
            sinb = p1.enter_context(nc.sbuf_tensor(f"sinb{b}", [128, T], BF16))
            p1b = p1.enter_context(nc.sbuf_tensor(f"p1b{b}", [128, nch * 128], BF16))
            p2b = p1.enter_context(nc.sbuf_tensor(f"p2b{b}", [128, nch * 128], BF16))
            cmb = p1.enter_context(nc.sbuf_tensor(f"cmb{b}", [128, QPG * TG], BF16))
            wA = p1.enter_context(nc.sbuf_tensor(f"wA{b}", [128, 12 * CH * 128], BF16))
            qT = p1.enter_context(nc.sbuf_tensor(f"qT{b}", [128, 4 * T], BF16))
            kT = p1.enter_context(nc.sbuf_tensor(f"kT{b}", [128, 4 * T], BF16))
            vsb = p1.enter_context(nc.sbuf_tensor(f"vsb{b}", [128, NTT * 520], BF16))
            rt = [p1.enter_context(nc.sbuf_tensor(f"rt{b}_{i}", [128, TG], BF16)) for i in range(4)]
            rqa = p1.enter_context(nc.sbuf_tensor(f"rqa{b}", [128, TG], BF16))
            rqb = p1.enter_context(nc.sbuf_tensor(f"rqb{b}", [128, TG], BF16))
            rox = [p1.enter_context(nc.sbuf_tensor(f"rox{b}_{i}", [128, TG], BF16)) for i in range(2)]
            Pb2 = [[p1.enter_context(nc.sbuf_tensor(f"Pb{b}_{i}_{j}", [128, T], BF16)) for j in range(3)] for i in range(2)]
            PTs = [p1.enter_context(nc.sbuf_tensor(f"PTs{b}_{i}", [128, 1024], BF16)) for i in range(3)]
            mb = p1.enter_context(nc.sbuf_tensor(f"mb{b}", [128, 48], F32))
            biasK = p1.enter_context(nc.sbuf_tensor(f"biasK{b}", [128, 48], F32))
            fbk = p1.enter_context(nc.sbuf_tensor(f"fbk{b}", [128, 48], F32))
            bcol = p1.enter_context(nc.sbuf_tensor(f"bcol{b}", [128, 16], F32))
            dsc_ = p1.enter_context(nc.sbuf_tensor(f"dscr{b}", [128, 16], F32))
            dbuf = p1.enter_context(nc.sbuf_tensor(f"dbuf{b}", [128, DV], F32))
            dbuf2 = p1.enter_context(nc.sbuf_tensor(f"dbuf2{b}", [128, DV], F32))
            diff2 = p1.enter_context(nc.sbuf_tensor(f"diff2{b}", [128, NTT * 128], BF16))
            dstg = [p1.enter_context(nc.sbuf_tensor(f"dstg{b}_{i}", [128, 1024], BF16)) for i in range(2)]

            wA_free = None
            BAR_P1 = S.barrier()
            S.dma("gpsimd", cosb[:], dram["cosr"], wait=BAR_P1, inc="dwa")
            S.dma("gpsimd", sinb[:], dram["sinr"], inc="dwa")
            S.dma("gpsimd", p1b[:], dram["permx1"], inc="dwa")
            S.dma("gpsimd", p2b[:], dram["permx2"], inc="dwa")
            S.dma("gpsimd", cmb[:], dram["cmask"], inc="dwa")
            TABS = lt(("dwa", S.counts["dwa"]))
            DIFF_EV = []
            dstg_free = [None, None]
            for SQ in range(NSQ):
                if NSQ > 1:
                    qcc = [SQ, 2 + SQ, 4 + SQ, 6 + SQ]
                    vc0 = 2 * nch * 128 + 512 * SQ
                else:
                    qcc = [0, 1, 2, 3]
                    vc0 = 2 * nch * 128
                col0 = [128 * cc for cc in qcc] + [nch * 128 + 128 * cc for cc in qcc] + \
                       [vc0 + 128 * j for j in range(4)]
                for j, c0 in enumerate(col0):
                    S.dma("gpsimd",
                          wA[:].rearrange("p (j c n) -> p j c n", j=12, c=CH)[:, j],
                          dram["wattn"][:, c0:c0 + 128].rearrange("(c p) n -> p c n", p=128),
                          wait=lt(wA_free, BAR_P1 if SQ == 0 else None), inc="dwa",
                          tag=f"wa{b}_{SQ}_{j}")
                WA = lt(("dwa", S.counts["dwa"]))

                def wa_tile(blk, c):
                    return wA[:, (blk * CH + c) * 128:(blk * CH + c) * 128 + 128]

                with ExitStack() as pq:
                    pAs = [pq.enter_context(nc.psum_tensor(f"pA{b}{SQ}_{i}", [128, 512], F32)) for i in range(2)]
                    pBs = [pq.enter_context(nc.psum_tensor(f"pB{b}{SQ}_{i}", [128, 512], F32)) for i in range(2)]
                    pP = pq.enter_context(nc.psum_tensor(f"pP{b}{SQ}", [128, 512], F32))
                    pVs = [pq.enter_context(nc.psum_tensor(f"pV{b}{SQ}_{i}", [128, 512], F32)) for i in range(2)]
                    pA_free, pB_free = [None, None], [None, None]
                    pP_free = None
                    pV_free = [None, None]
                    rqa_free, rqb_free = None, None
                    rt_free = [None] * 4
                    rox_free = [None, None]
                    for ti, dst in ((0, qT), (1, kT)):
                        for pr in range(2):
                            x1b, x2b = 4 * ti + pr, 4 * ti + pr + 2
                            for tg in range(NTG):
                                qsl = tg % 2
                                pA, pB = pAs[qsl], pBs[qsl]
                                def mmA(e, x1b=x1b, tg=tg, pA=pA):
                                    last = None
                                    for c in range(CH):
                                        last = e.matmul(pA[:, 0:TG], wa_tile(x1b, c),
                                                        xbf_sl(c, TG * tg, TG),
                                                        start=(c == 0), stop=(c == CH - 1))
                                    return last
                                tA = S.step("tensor", mmA, wait=lt(WA, XBF_READY, pA_free[qsl]), inc="t")
                                def mmB(e, x2b=x2b, tg=tg, pB=pB):
                                    last = None
                                    for c in range(CH):
                                        last = e.matmul(pB[:, 0:TG], wa_tile(x2b, c),
                                                        xbf_sl(c, TG * tg, TG),
                                                        start=(c == 0), stop=(c == CH - 1))
                                    return last
                                tB = S.step("tensor", mmB, wait=lt(pB_free[qsl]), inc="t")
                                evA = S.step("scalar", lambda e, pA=pA: e.copy(
                                    rqa[:], pA[:, 0:TG]), wait=lt([tA], rqa_free), inc="s")
                                evB = S.step("scalar", lambda e, pB=pB: e.copy(
                                    rqb[:], pB[:, 0:TG]), wait=lt([tB], rqb_free), inc="s")
                                pA_free[qsl] = evA
                                pB_free[qsl] = evB
                                v1 = S.step("vector", lambda e, tg=tg: e.tensor_tensor(
                                    rt[0][:], rqa[:], cosb[:, TG * tg:TG * tg + TG], op=OP.mult),
                                    wait=lt([evA], rt_free[0]), inc="v")
                                v2 = S.step("gpsimd", lambda e, tg=tg: e.tensor_tensor(
                                    rt[1][:], rqb[:], sinb[:, TG * tg:TG * tg + TG], op=OP.mult),
                                    wait=lt([evB], rt_free[1]), inc="p")
                                v3 = S.step("vector", lambda e, tg=tg: e.tensor_tensor(
                                    rt[2][:], rqa[:], sinb[:, TG * tg:TG * tg + TG], op=OP.mult),
                                    wait=lt([evA], rt_free[2], [v1]), inc="v")
                                v4 = S.step("gpsimd", lambda e, tg=tg: e.tensor_tensor(
                                    rt[3][:], rqb[:], cosb[:, TG * tg:TG * tg + TG], op=OP.mult),
                                    wait=lt([evB], rt_free[3], [v2]), inc="p")
                                rqa_free = lt(v1, v3)
                                rqb_free = lt(v2, v4)
                                v5 = S.step("vector", lambda e: e.tensor_tensor(
                                    rox[0][:], rt[0][:], rt[1][:], op=OP.subtract),
                                    wait=lt(v1, v2, rox_free[0]), inc="v")
                                v6 = S.step("gpsimd", lambda e: e.tensor_tensor(
                                    rox[1][:], rt[2][:], rt[3][:], op=OP.add),
                                    wait=lt(v3, v4, rox_free[1]), inc="p")
                                rt_free = [v5, v5, v6, v6]
                                ev_toks = []
                                for k2 in range(2):
                                    if NSQ > 1:
                                        gc = (2 * SQ + k2) if pr == 0 else (4 + 2 * SQ + k2)
                                        lc = k2 if pr == 0 else 2 + k2
                                    else:
                                        gc = 2 * pr + k2
                                        lc = gc
                                    def pmm(e, gc=gc):
                                        e.matmul(pP[:, 0:TG], p1b[:, 128 * gc:128 * gc + 128],
                                                 rox[0][:], start=True, stop=False)
                                        return e.matmul(pP[:, 0:TG], p2b[:, 128 * gc:128 * gc + 128],
                                                        rox[1][:], start=False, stop=True)
                                    tP = S.step("tensor", pmm, wait=lt(v5, v6, pP_free), inc="t")
                                    evP = S.step("scalar",
                                                 lambda e, dst=dst, lc=lc, tg=tg: e.copy(
                                                     dst[:, lc * T + TG * tg: lc * T + TG * tg + TG],
                                                     pP[:, 0:TG]),
                                                 wait=[tP], inc="s")
                                    pP_free = evP
                                    ev_toks.append(evP)
                                rox_free = [lt(*ev_toks), lt(*ev_toks)]
                    QK_READY = lt(("s", S.counts["s"]))

                    wa3 = wA[:].rearrange("p (j c n) -> p j c n", j=12, c=CH)
                    for tt in range(NTT):
                        vsl = tt % 2
                        pV = pVs[vsl]
                        def vmm(e, tt=tt, pV=pV):
                            last = None
                            for c in range(CH):
                                rhs = wa3[:, 8:12, c, :]
                                last = e.matmul(pV[:, 0:512], xbf_sl(c, 128 * tt, 128), rhs,
                                                start=(c == 0), stop=(c == CH - 1))
                            return last
                        tV = S.step("tensor", vmm, wait=lt(WA, XBF_READY, pV_free[vsl]), inc="t")
                        evV = S.step("vector",
                                     lambda e, tt=tt, pV=pV: e.tensor_copy(
                                         vsb[:, 520 * tt:520 * tt + 520].rearrange(
                                             "p (h n) -> p h n", h=8)[:, :, 0:64],
                                         pV[:, 0:512].rearrange("p (h n) -> p h n", h=8)),
                                     wait=[tV], inc="v")
                        msV = S.step("vector",
                                     lambda e, tt=tt: e.memset(
                                         vsb[:, 520 * tt:520 * tt + 520].rearrange(
                                             "p (h n) -> p h n", h=8)[:, :, 64:65], 1.0),
                                     wait=[tV], inc="v")
                        pV_free[vsl] = evV
                    V_READY = lt(("v", S.counts["v"]))

                with ExitStack() as ps_:
                    psS = [ps_.enter_context(nc.psum_tensor(f"psS{b}{SQ}_{i}", [128, 512], F32)) for i in range(3)]
                    psPT = [ps_.enter_context(nc.psum_tensor(f"psPT{b}{SQ}_{i}", [128, 1024], BF16)) for i in range(3)]
                    psA = [ps_.enter_context(nc.psum_tensor(f"psA{b}{SQ}_{i}", [128, 512], F32)) for i in range(2)]
                    sS_free = [None] * 3
                    sPT_free = [None] * 3
                    sA_free = [None, None]
                    PTs_free = [None] * 3
                    sctr = [0]
                    tctr = [0]
                    Pb_free = [[None] * 3, [None] * 3]
                    mb_free = [None] * 6
                    bcol_free = [None] * 12
                    dsc_free = None
                    d2w = [[None, None] for _ in range(NTT)]
                    d2r = None
                    biasTok = [[None] * 3, [None] * 3]
                    expTok = [[None] * 3, [None] * 3]
                    fbk_free = [None] * 6

                    items = [(lhp, qt) for lhp in range(HPSQ) for qt in range(NTT)]

                    def hp_info(lhp):
                        hp = HPSQ * SQ + lhp
                        base = 32 * (hp % 4)
                        lc_a = (hp // 4 - 2 * SQ) if NSQ > 1 else hp // 4
                        return hp, base, lc_a, lc_a + 2

                    def stage_AB_gen(i):
                        lhp, qt = items[i]
                        hp, base, lc_a, lc_a2 = hp_info(lhp)
                        par = i % 3
                        nkg = qt // QPG + 1
                        for hi, lc in ((0, lc_a), (1, lc_a2)):
                            idx8 = (3 * hi + par) * 8
                            e_toks = []
                            red_toks = []
                            for kg in range(nkg):
                                bk = sctr[0] % 3
                                sctr[0] += 1
                                def smm(e, lc=lc, qt=qt, kg=kg, bk=bk, base=base):
                                    return e.matmul(
                                        psS[bk][:, 0:TG],
                                        qT[base:base + 32, lc * T + 128 * qt: lc * T + 128 * qt + 128],
                                        kT[base:base + 32, lc * T + TG * kg: lc * T + TG * kg + TG],
                                        start=True, stop=True, tile_position=(base, 0))
                                tS = S.step("tensor", smm, wait=lt(QK_READY, sS_free[bk]), inc="t")
                                yield
                                rd = S.step("vector",
                                            lambda e, kg=kg, bk=bk, idx8=idx8: e.reduce_max(
                                                mb[:, idx8 + kg:idx8 + kg + 1],
                                                psS[bk][:, 0:TG], axis=AX.X, negate=True),
                                            wait=lt([tS], mb_free[3 * hi + par]
                                                    if kg == 0 else None), inc="v")
                                bs_ = S.step("vector",
                                             lambda e, kg=kg, idx8=idx8: e.tensor_scalar_mul(
                                                 biasK[:, idx8 + kg:idx8 + kg + 1],
                                                 mb[:, idx8 + kg:idx8 + kg + 1], float(SMS)),
                                             wait=[rd], inc="v")
                                ex = S.step("scalar",
                                            lambda e, hi=hi, par=par, kg=kg, bk=bk, idx8=idx8:
                                            e.activation(
                                                Pb2[hi][par][:, TG * kg:TG * kg + TG],
                                                psS[bk][:, 0:TG],
                                                AF.Exp, bias=biasK[:, idx8 + kg:idx8 + kg + 1],
                                                scale=float(SMS)),
                                            wait=lt([bs_], Pb_free[hi][par]), inc="s")
                                sS_free[bk] = ex
                                e_toks.append(ex)
                                red_toks.append(rd)
                            bi = 3 * hi + par
                            cm1 = S.step("vector",
                                         lambda e, nkg=nkg, bi=bi, idx8=idx8: e.tensor_reduce(
                                             bcol[:, bi:bi + 1], mb[:, idx8:idx8 + nkg],
                                             axis=AX.X, op=OP.min),
                                         wait=lt(*red_toks, bcol_free[bi]), inc="v")
                            mb_free[bi] = cm1
                            cm2 = S.step("vector",
                                         lambda e, bi=bi: e.tensor_scalar_mul(
                                             bcol[:, 6 + bi:7 + bi], bcol[:, bi:bi + 1], float(SMS)),
                                         wait=lt([cm1], bcol_free[6 + bi]), inc="v")
                            biasTok[hi][par] = cm2
                            if nkg > 1:
                                fb1 = S.step("vector",
                                             lambda e, idx8=idx8, nkg=nkg, bi=bi: e.tensor_scalar(
                                                 fbk[:, idx8:idx8 + nkg], biasK[:, idx8:idx8 + nkg],
                                                 bcol[:, 6 + bi:7 + bi], None, op0=OP.subtract),
                                             wait=lt([cm2], fbk_free[bi]), inc="v")
                                fb2 = S.step("scalar",
                                             lambda e, idx8=idx8, nkg=nkg: e.activation(
                                                 fbk[:, idx8:idx8 + nkg], fbk[:, idx8:idx8 + nkg],
                                                 AF.Exp, scale=-1.0),
                                             wait=[fb1], inc="s")
                                bcol_free[6 + bi] = fb1
                                rs_toks = []
                                for kg in range(nkg):
                                    rs = S.step("gpsimd",
                                                lambda e, hi=hi, par=par, kg=kg, idx8=idx8:
                                                e.tensor_tensor(
                                                    Pb2[hi][par][:, TG * kg:TG * kg + TG],
                                                    Pb2[hi][par][:, TG * kg:TG * kg + TG],
                                                    fbk[:, idx8 + kg:idx8 + kg + 1]
                                                    .to_broadcast([128, TG]),
                                                    op=OP.mult),
                                                wait=lt([fb2], e_toks[kg]), inc="p")
                                    rs_toks.append(rs)
                                mk = S.step("gpsimd",
                                            lambda e, hi=hi, par=par, qt=qt, nkg=nkg: e.tensor_tensor(
                                                Pb2[hi][par][:, TG * (nkg - 1):TG * nkg],
                                                Pb2[hi][par][:, TG * (nkg - 1):TG * nkg],
                                                cmb[:, TG * (qt % QPG):TG * (qt % QPG) + TG],
                                                op=OP.mult),
                                            wait=lt(rs_toks[-1]), inc="p")
                                fbk_free[bi] = lt(mk, *rs_toks)
                                expTok[hi][par] = lt(mk, *rs_toks)
                            else:
                                bcol_free[6 + bi] = cm2
                                mk = S.step("gpsimd",
                                            lambda e, hi=hi, par=par, qt=qt, nkg=nkg: e.tensor_tensor(
                                                Pb2[hi][par][:, TG * (nkg - 1):TG * nkg],
                                                Pb2[hi][par][:, TG * (nkg - 1):TG * nkg],
                                                cmb[:, TG * (qt % QPG):TG * (qt % QPG) + TG],
                                                op=OP.mult),
                                            wait=lt(*e_toks), inc="p")
                                expTok[hi][par] = lt(mk, *e_toks)

                    def stage_C_gen(i):
                        nonlocal dsc_free, d2r
                        lhp, qt = items[i]
                        hp, base, lc_a, lc_a2 = hp_info(lhp)
                        par = i % 3
                        aslot = i % 2
                        nkt = qt + 1
                        nkg = qt // QPG + 1
                        vh = lhp
                        pv_last = []
                        batches = []
                        for hi in (0, 1):
                            for k0 in range(0, nkt, 8):
                                batches.append((hi, list(range(k0, min(k0 + 8, nkt)))))
                        pend_pv = []
                        pv_tok = {0: None, 1: None}
                        first_pv = {0: True, 1: True}
                        last_tT = {0: None, 1: None}

                        def emit_trp(hi, kts):
                            tsl = tctr[0] % 3
                            tctr[0] += 1
                            ealt = tctr[0] % 2
                            def trp(e, hi=hi, par=par, kts=tuple(kts), tsl=tsl):
                                last = None
                                for i_, kt in enumerate(kts):
                                    last = e.transpose(
                                        psPT[tsl][:, 128 * i_:128 * i_ + 128],
                                        Pb2[hi][par][:, 128 * kt:128 * kt + 128], identb[:])
                                return last
                            tT = S.step("tensor", trp,
                                        wait=lt(expTok[hi][par], sPT_free[tsl]), inc="t")
                            last_tT[hi] = tT
                            if ealt == 0:
                                evT = S.step("scalar",
                                             lambda e, tsl=tsl, n=len(kts): e.copy(
                                                 PTs[tsl][:, 0:128 * n], psPT[tsl][:, 0:128 * n]),
                                             wait=lt([tT], PTs_free[tsl]), inc="s")
                            else:
                                evT = S.step("vector",
                                             lambda e, tsl=tsl, n=len(kts): e.tensor_copy(
                                                 PTs[tsl][:, 0:128 * n], psPT[tsl][:, 0:128 * n]),
                                             wait=lt([tT], PTs_free[tsl]), inc="v")
                            sPT_free[tsl] = evT
                            return evT, tsl

                        def emit_pv(hi, kts, evT, tsl):
                            def pvm(e, kts=tuple(kts), tsl=tsl, vh=vh, hi=hi,
                                    aslot=aslot, first=first_pv[hi], nkt=nkt):
                                last = None
                                for i_, kt in enumerate(kts):
                                    last = e.matmul(
                                        psA[aslot][:, 65 * hi:65 * hi + 65],
                                        PTs[tsl][:, 128 * i_:128 * i_ + 128],
                                        vsb[:, 520 * kt + 65 * vh:520 * kt + 65 * vh + 65],
                                        start=(first and i_ == 0),
                                        stop=(kt == nkt - 1))
                                return last
                            tPV = S.step("tensor", pvm,
                                         wait=lt([evT], V_READY,
                                                 sA_free[aslot] if first_pv[hi] else None),
                                         inc="t")
                            PTs_free[tsl] = tPV
                            first_pv[hi] = False
                            pv_tok[hi] = tPV

                        pend_pv = []
                        bi_ = 0
                        while bi_ < len(batches) or pend_pv:
                            if bi_ < len(batches) and len(pend_pv) < 2:
                                hi, kts = batches[bi_]
                                evT, tsl = emit_trp(hi, kts)
                                pend_pv.append((hi, kts, evT, tsl))
                                bi_ += 1
                            else:
                                hi, kts, evT, tsl = pend_pv.pop(0)
                                emit_pv(hi, kts, evT, tsl)
                            yield
                        for hi in (0, 1):
                            Pb_free[hi][par] = last_tT[hi]
                            pv_last.append(pv_tok[hi])
                        PVD = lt(*pv_last)

                        c1 = S.step("vector", lambda e, aslot=aslot: (
                            e.reciprocal(dsc_[:, 0:1], psA[aslot][:, 64:65]),
                            e.reciprocal(dsc_[:, 1:2], psA[aslot][:, 129:130]))[-1],
                            wait=lt(PVD, LAM, dsc_free), inc="v")
                        c2 = S.step("vector", lambda e, aslot=aslot: (
                            e.tensor_tensor(dsc_[:, 2:3], dsc_[:, 1:2], lamv[:, 5:6], op=OP.mult),
                            e.tensor_scalar(dbuf[:], psA[aslot][:, 0:64], dsc_[:, 0:1], None,
                                            op0=OP.mult))[-1],
                            wait=[c1], inc="v")
                        c3 = S.step("vector", lambda e, aslot=aslot: e.scalar_tensor_tensor(
                            dbuf2[:], psA[aslot][:, 65:129], dsc_[:, 2:3], dbuf[:],
                            op0=OP.mult, op1=OP.add),
                            wait=[c2], inc="v")
                        c4 = S.step("scalar", lambda e: e.activation(
                            dbuf[:], dbuf2[:], AF.Square, accum_out=dsc_[:, 4:5]),
                            wait=[c3], inc="s")
                        c7a = S.step("scalar", lambda e: e.activation(
                            dsc_[:, 5:6], dsc_[:, 4:5], AF.Ln, scale=1.0 / DV, bias=epsc[:]),
                            wait=[c4], inc="s")
                        c7 = S.step("scalar", lambda e: e.activation(
                            dsc_[:, 6:7], dsc_[:, 5:6], AF.Exp, scale=-0.5),
                            wait=[c7a], inc="s")
                        sA_free[aslot] = c3
                        c9 = S.step("vector",
                                    lambda e, qt=qt, lhp=lhp: e.tensor_scalar(
                                        diff2[:, 128 * qt + 64 * (lhp % 2):
                                              128 * qt + 64 * (lhp % 2) + 64],
                                        dbuf2[:], dsc_[:, 6:7], float(1.0 - cfg["LAM_INIT"]),
                                        op0=OP.mult, op1=OP.mult),
                                    wait=lt([c7], d2r), inc="v")
                        dsc_free = c9
                        d2w[qt][lhp % 2] = c9

                        if lhp % 2 == 1 and qt == NTT - 1:
                            gchunk = hp // 2
                            last_tD = None
                            for t0 in range(0, NTT, 8):
                                n = min(8, NTT - t0)
                                tsl = tctr[0] % 3
                                tctr[0] += 1
                                def dtr(e, t0=t0, n=n, tsl=tsl):
                                    last = None
                                    for i_ in range(n):
                                        last = e.transpose(
                                            psPT[tsl][:, 128 * i_:128 * i_ + 128],
                                            diff2[:, 128 * (t0 + i_):128 * (t0 + i_) + 128],
                                            identb[:])
                                    return last
                                dwaits = lt(*[tok for i_ in range(n) for tok in d2w[t0 + i_]],
                                            sPT_free[tsl])
                                tD = S.step("tensor", dtr, wait=dwaits, inc="t")
                                dsl = (t0 // 8) % 2
                                evD = S.step("vector",
                                             lambda e, n=n, tsl=tsl, dsl=dsl: e.tensor_copy(
                                                 dstg[dsl][:, 0:128 * n], psPT[tsl][:, 0:128 * n]),
                                             wait=lt([tD], dstg_free[dsl]), inc="v")
                                sPT_free[tsl] = evD
                                ddma = S.dma("sync",
                                             dT_scr[b].rearrange("(c p) t -> p c t", p=128)
                                             [:, gchunk, 128 * t0:128 * t0 + 128 * n],
                                             dstg[dsl][:, 0:128 * n],
                                             wait=[evD], inc=f"dd{dsl}")
                                dstg_free[dsl] = ddma
                                DIFF_EV.append(ddma)
                                last_tD = tD
                            d2r = last_tD

                    NI = len(items)
                    def drain(g):
                        if g is None:
                            return None
                        try:
                            next(g)
                            return g
                        except StopIteration:
                            return None
                    for i in range(NI + 2):
                        g_ab = stage_AB_gen(i) if i < NI else None
                        g_c = stage_C_gen(i - 2) if i >= 2 else None
                        while g_ab is not None or g_c is not None:
                            g_ab = drain(g_ab)
                            g_c = drain(g_c)
                            g_c = drain(g_c)
                wA_free = lt(("t", S.counts["t"]))
            DIFFT_READY = lt(("dd0", S.counts.get("dd0", 0)), ("dd1", S.counts.get("dd1", 0)))
        s_x.close()

        y32 = bs.enter_context(nc.sbuf_tensor(f"y32{b}", [128, CH * T], F32))
        ybf = bs.enter_context(nc.sbuf_tensor(f"ybf{b}", [128, CH * T], BF16))

        def y32_sl(c, t0, n):
            return y32[:, c * T + t0: c * T + t0 + n]

        def ybf_sl(c, t0, n):
            return ybf[:, c * T + t0: c * T + t0 + n]

        with ExitStack() as p2:
            wp = p2.enter_context(nc.sbuf_tensor(f"wp{b}", [128, CH * C], BF16))
            xts = p2.enter_context(nc.sbuf_tensor(f"xts{b}", [128, CH * TG], F32))
            dts = [p2.enter_context(nc.sbuf_tensor(f"dts{b}_{i}", [128, CH * TG], BF16)) for i in range(2)]
            lnb = alloc_ln_bufs(nc, p2, b, 0, TG)
            pat = [p2.enter_context(nc.psum_tensor(f"pat{b}_{i}", [128, 512], F32)) for i in range(2)]
            pstA = p2.enter_context(nc.psum_tensor(f"pstA{b}", [128, 512], F32))
            pstB = p2.enter_context(nc.psum_tensor(f"pstB{b}", [128, 512], F32))
            pbc = p2.enter_context(nc.psum_tensor(f"pbc{b}", [128, 1024], F32))

            BAR_P2 = S.barrier()
            S.dma("gpsimd", wp[:].rearrange("p (c n) -> p c n", c=CH),
                  dram["wproj"].rearrange("(c p) n -> p c n", p=128),
                  wait=lt(CONST, BAR_P2), inc="dwa")
            WP = lt(("dwa", S.counts["dwa"]))
            xts_free = None
            dts_free = [None, None]
            pat_free = [None, None]
            ln_state = dict(prev=None)
            for tg in range(NTG):
                dsl2 = tg % 2
                ddt = S.dma("sync",
                            dts[dsl2][:].rearrange("p (c n) -> p c n", c=CH),
                            dT_scr[b].rearrange("(c p) t -> p c t", p=128)[:, :, TG * tg:TG * tg + TG],
                            wait=lt(DIFFT_READY, dts_free[dsl2], BAR_P2 if tg < 2 else None),
                            inc=f"dd{dsl2}")
                dxt = S.dma("sync",
                            xts[:].rearrange("p (c n) -> p c n", c=CH),
                            xt_scr[b].rearrange("(c p) t -> p c t", p=128)[:, :, TG * tg:TG * tg + TG],
                            wait=lt(XSCR_DONE, xts_free, BAR_P2 if tg == 0 else None), inc="dxs")
                xa_toks = []
                for cc in range(CH):
                    bsl = cc % 2
                    def wpm(e, cc=cc, dsl2=dsl2, bsl=bsl):
                        last = None
                        for c in range(CH):
                            last = e.matmul(pat[bsl][:, 0:TG],
                                            wp[:, c * C + 128 * cc: c * C + 128 * cc + 128],
                                            dts[dsl2][:, c * TG:c * TG + TG],
                                            start=(c == 0), stop=(c == CH - 1))
                        return last
                    tw = S.step("tensor", wpm, wait=lt(WP, [ddt], pat_free[bsl]), inc="t")
                    xa = S.step("vector",
                                lambda e, cc=cc, bsl=bsl: e.tensor_tensor(
                                    xts[:, cc * TG:cc * TG + TG], xts[:, cc * TG:cc * TG + TG],
                                    pat[bsl][:, 0:TG], op=OP.add),
                                wait=lt([tw], [dxt]), inc="v")
                    pat_free[bsl] = xa
                    xa_toks.append(xa)
                    dts_free[dsl2] = tw
                ln_last = emit_ln(nc, S, cfg, tg, lambda c, tg=tg: xts[:, c * TG:c * TG + TG],
                                  y32_sl, ybf_sl, lnb, pstA, pstB, pbc, olnb, ob32,
                                  xa_toks, ln_state)
                xts_free = ln_last
            Y_READY = S.barrier()

        s_f = ExitStack()
        coefT = s_f.enter_context(nc.sbuf_tensor(f"coefT{b}", [64, T], BF16))
        sg_ = [s_f.enter_context(nc.sbuf_tensor(f"sgb{b}_{i}", [128, TG], F32)) for i in range(2)]
        tm1 = [s_f.enter_context(nc.sbuf_tensor(f"tm1{b}_{i}", [128, TG], BF16)) for i in range(2)]
        selb = s_f.enter_context(nc.sbuf_tensor(f"selb{b}", [64, 32 * 128], BF16))
        wg = [s_f.enter_context(nc.sbuf_tensor(f"wg{b}_{i}", [128, CH * I], BF16)) for i in range(2)]
        wu = [s_f.enter_context(nc.sbuf_tensor(f"wu{b}_{i}", [128, CH * I], BF16)) for i in range(2)]
        wd = [s_f.enter_context(nc.sbuf_tensor(f"wd{b}_{i}", [128, IC * C], BF16)) for i in range(2)]
        hT = s_f.enter_context(nc.sbuf_tensor(f"hT{b}", [128, 2 * IC * TG], BF16))
        coefb = [s_f.enter_context(nc.sbuf_tensor(f"coefb{b}_{i}", [128, TG], BF16)) for i in range(2)]
        pgu = [s_f.enter_context(nc.psum_tensor(f"pgu{b}_{i}", [128, 512], F32)) for i in range(4)]

        NEXP = E + 2

        def wsrc(e_i, which):
            if e_i < E:
                return dram[which][e_i]
            h_ = e_i - E
            if which == "Wg":
                return dram["SWg"][:, I * h_:I * (h_ + 1)]
            if which == "Wu":
                return dram["SWu"][:, I * h_:I * (h_ + 1)]
            return dram["SWd"][I * h_:I * (h_ + 1), :]

        BAR_F = S.barrier()
        ct_init = S.step("vector", lambda e: e.memset(coefT[:], 0.0), wait=BAR_F, inc="v")
        S.dma("gpsimd", selb[:], dram["sel_bank"], wait=BAR_F, inc="dwa")
        SEL = lt(("dwa", S.counts["dwa"]))
        wload_g = [None, None]
        wload_u = [None, None]
        wload_d = [None, None]

        def emit_wdma(e_i, wait=None):
            par = e_i % 2
            t1 = S.dma("gpsimd", wg[par][:].rearrange("p (c n) -> p c n", c=CH),
                       wsrc(e_i, "Wg").rearrange("(c p) n -> p c n", p=128),
                       wait=lt(wait), inc=f"dwg{par}")
            t2 = S.dma("gpsimd", wu[par][:].rearrange("p (c n) -> p c n", c=CH),
                       wsrc(e_i, "Wu").rearrange("(c p) n -> p c n", p=128), inc=f"dwu{par}")
            t3 = S.dma("gpsimd", wd[par][:].rearrange("p (c n) -> p c n", c=IC),
                       wsrc(e_i, "Wd").rearrange("(c p) n -> p c n", p=128), inc=f"dwd{par}")
            wload_g[par], wload_u[par], wload_d[par] = t1, t2, t3

        emit_wdma(0, wait=BAR_F)
        emit_wdma(1, wait=BAR_F)

        with ExitStack() as p3:
            rws = p3.enter_context(nc.sbuf_tensor(f"rws{b}", [E, C], F32))
            rwTh = p3.enter_context(nc.sbuf_tensor(f"rwTh{b}", [128, CH * E], BF16))
            rwTl = p3.enter_context(nc.sbuf_tensor(f"rwTl{b}", [128, CH * E], BF16))
            ylo = [p3.enter_context(nc.sbuf_tensor(f"ylo{b}_{i}", [128, 128], BF16)) for i in range(2)]
            rsc = p3.enter_context(nc.sbuf_tensor(f"rsc{b}", [128, 9 * E + 2], F32))
            coefa = p3.enter_context(nc.sbuf_tensor(f"coefa{b}", [128, NTT * E], F32))
            m8b = p3.enter_context(nc.sbuf_tensor(f"m8b{b}", [128, 16], F32))
            prt = p3.enter_context(nc.psum_tensor(f"prt{b}", [128, 512], F32))
            prl = p3.enter_context(nc.psum_tensor(f"prl{b}", [128, 512], F32))
            pct = p3.enter_context(nc.psum_tensor(f"pct{b}", [128, 1024], F32))

            drw = S.dma("sync", rws[:], dram["router_w"], wait=lt(CONST, BAR_F), inc="dx")
            rtr_toks = []
            prt_free = None
            for cg in range(0, CH, 8):
                n = min(8, CH - cg)
                def rtr(e, cg=cg, n=n):
                    last = None
                    for i_ in range(n):
                        last = e.transpose(prt[:, E * i_:E * i_ + E],
                                           rws[0:E, 128 * (cg + i_):128 * (cg + i_) + 128],
                                           identf[0:E, 0:E])
                    return last
                tR = S.step("tensor", rtr, wait=lt([drw], CONST, prt_free), inc="t")
                evh = S.step("scalar",
                             lambda e, cg=cg, n=n: e.copy(rwTh[:, E * cg:E * cg + E * n],
                                                          prt[:, 0:E * n]),
                             wait=[tR], inc="s")
                evl = S.step("vector",
                             lambda e, cg=cg, n=n: e.tensor_tensor(
                                 rwTl[:, E * cg:E * cg + E * n], prt[:, 0:E * n],
                                 rwTh[:, E * cg:E * cg + E * n], op=OP.subtract),
                             wait=[evh], inc="v")
                prt_free = evl
                rtr_toks.append(evl)
            RWT = lt(*rtr_toks)
            ylo_free = [None, None]
            ct_toks = []
            rsc_free = None
            prl_free = None
            for tt in range(NTT):
                yl_toks = []
                mm_toks = []
                for c in range(CH):
                    sl = c % 2
                    yl = S.step("vector",
                                lambda e, c=c, tt=tt, sl=sl: e.tensor_tensor(
                                    ylo[sl][:], y32_sl(c, 128 * tt, 128), ybf_sl(c, 128 * tt, 128),
                                    op=OP.subtract),
                                wait=lt(Y_READY, ylo_free[sl]), inc="v")
                    def lmm(e, c=c, tt=tt, sl=sl):
                        e.matmul(prl[:, 0:E], ybf_sl(c, 128 * tt, 128), rwTh[:, E * c:E * c + E],
                                 start=(c == 0), stop=False)
                        e.matmul(prl[:, 0:E], ybf_sl(c, 128 * tt, 128), rwTl[:, E * c:E * c + E],
                                 start=False, stop=False)
                        return e.matmul(prl[:, 0:E], ylo[sl][:], rwTh[:, E * c:E * c + E],
                                        start=False, stop=(c == CH - 1))
                    tl = S.step("tensor", lmm,
                                wait=lt([yl], RWT, prl_free if c == 0 else None), inc="t")
                    ylo_free[sl] = tl
                    mm_toks.append(tl)
                sg1 = S.step("scalar",
                             lambda e: e.activation(rsc[:, 0:E], prl[:, 0:E], AF.Exp, scale=-1.0),
                             wait=lt(lt(*mm_toks), rsc_free), inc="s")
                prl_free = sg1
                r1 = S.step("vector", lambda e: e.tensor_scalar(
                    rsc[:, E:2 * E], rsc[:, 0:E], 1.0, None, op0=OP.add), wait=[sg1], inc="v")
                r2 = S.step("vector", lambda e: e.reciprocal(
                    rsc[:, 0:E], rsc[:, E:2 * E]), wait=[r1], inc="v")
                r3 = S.step("vector", lambda e: e.tensor_tensor(
                    rsc[:, E:2 * E], rsc[:, 0:E], ebr[:, 0:E], op=OP.add), wait=lt([r2], CONST), inc="v")
                r4 = S.step("vector", lambda e: e.tensor_reduce(
                    rsc[:, 2 * E:2 * E + EG],
                    rsc[:, E:2 * E].rearrange("p (g k) -> p g k", g=EG),
                    axis=AX.X, op=OP.max), wait=[r3], inc="v")
                r5 = S.step("vector", lambda e: e.match_replace(
                    out=rsc[:, 3 * E:4 * E], in_to_replace=rsc[:, 2 * E:2 * E + EG],
                    in_values=rsc[:, E:2 * E], imm_value=-1e30), wait=[r4], inc="v")
                r6 = S.step("vector", lambda e: e.tensor_reduce(
                    rsc[:, 2 * E + EG:2 * E + 2 * EG],
                    rsc[:, 3 * E:4 * E].rearrange("p (g k) -> p g k", g=EG),
                    axis=AX.X, op=OP.max), wait=[r5], inc="v")
                r7 = S.step("vector", lambda e: e.tensor_tensor(
                    rsc[:, 2 * E:2 * E + EG], rsc[:, 2 * E:2 * E + EG],
                    rsc[:, 2 * E + EG:2 * E + 2 * EG], op=OP.add), wait=[r6], inc="v")
                r8 = S.step("vector", lambda e: e.max(out=m8b[:, 0:8],
                                                      in_=rsc[:, 2 * E:2 * E + EG]),
                            wait=[r7], inc="v")
                r9 = S.step("vector", lambda e: e.tensor_scalar(
                    rsc[:, 2 * E + EG:2 * E + 2 * EG], rsc[:, 2 * E:2 * E + EG],
                    m8b[:, 3:4], None, op0=OP.is_ge), wait=[r8], inc="v")
                r10 = S.step("vector", lambda e: e.tensor_copy(
                    rsc[:, 3 * E:4 * E].rearrange("p (g k) -> p g k", g=EG),
                    rsc[:, 2 * E + EG:2 * E + 2 * EG].rearrange("p (g k) -> p g k", k=1)
                    .to_broadcast([128, EG, EPG])), wait=[r9], inc="v")
                r11 = S.step("vector", lambda e: e.tensor_tensor(
                    rsc[:, 4 * E:5 * E], rsc[:, E:2 * E], rsc[:, 3 * E:4 * E], op=OP.mult),
                    wait=[r10], inc="v")
                r12 = S.step("vector", lambda e: e.tensor_scalar(
                    rsc[:, 5 * E:6 * E], rsc[:, 3 * E:4 * E], -1.0, 1.0,
                    op0=OP.mult, op1=OP.add), wait=[r10], inc="v")
                r13 = S.step("vector", lambda e: e.tensor_tensor(
                    rsc[:, 5 * E:6 * E], rsc[:, 5 * E:6 * E], nib[:, 0:E], op=OP.mult),
                    wait=[r12], inc="v")
                r14 = S.step("vector", lambda e: e.tensor_tensor(
                    rsc[:, 4 * E:5 * E], rsc[:, 4 * E:5 * E], rsc[:, 5 * E:6 * E], op=OP.add),
                    wait=lt(r11, r13), inc="v")
                r15 = S.step("vector", lambda e: e.max(out=m8b[:, 8:16],
                                                       in_=rsc[:, 4 * E:5 * E]),
                             wait=[r14], inc="v")
                r16 = S.step("vector", lambda e: e.tensor_scalar(
                    rsc[:, 6 * E:7 * E], rsc[:, 4 * E:5 * E], m8b[:, 15:16], None,
                    op0=OP.is_ge), wait=[r15], inc="v")
                r17 = S.step("vector", lambda e: e.tensor_tensor(
                    rsc[:, 7 * E:8 * E], rsc[:, 0:E], rsc[:, 6 * E:7 * E], op=OP.mult),
                    wait=[r16], inc="v")
                r18 = S.step("vector", lambda e: e.tensor_reduce(
                    rsc[:, 8 * E:8 * E + 1], rsc[:, 7 * E:8 * E], axis=AX.X, op=OP.add),
                    wait=[r17], inc="v")
                r19 = S.step("vector", lambda e: e.tensor_scalar(
                    rsc[:, 8 * E:8 * E + 1], rsc[:, 8 * E:8 * E + 1], 1e-20, None, op0=OP.add),
                    wait=[r18], inc="v")
                r20 = S.step("vector", lambda e: e.reciprocal(
                    rsc[:, 8 * E + 1:8 * E + 2], rsc[:, 8 * E:8 * E + 1]), wait=[r19], inc="v")
                r21 = S.step("vector", lambda e: e.tensor_scalar(
                    rsc[:, 8 * E + 1:8 * E + 2], rsc[:, 8 * E + 1:8 * E + 2],
                    float(cfg["SCALE_MOE"]), None, op0=OP.mult), wait=[r20], inc="v")
                r22 = S.step("vector", lambda e, tt=tt: e.tensor_scalar(
                    coefa[:, E * tt:E * tt + E], rsc[:, 7 * E:8 * E],
                    rsc[:, 8 * E + 1:8 * E + 2], None, op0=OP.mult), wait=[r21], inc="v")
                rsc_free = r22
                ct_toks.append(r22)
            pct_free = None
            for t0 in range(0, NTT, 8):
                n = min(8, NTT - t0)
                def ctr(e, t0=t0, n=n):
                    last = None
                    for i_ in range(n):
                        last = e.transpose(pct[0:E, 128 * i_:128 * i_ + 128],
                                           coefa[:, E * (t0 + i_):E * (t0 + i_) + E],
                                           identf[:])
                    return last
                tC = S.step("tensor", ctr, wait=lt(*ct_toks[t0:t0 + n], pct_free), inc="t")
                evC = S.step("scalar",
                             lambda e, t0=t0, n=n: e.copy(coefT[0:E, 128 * t0:128 * t0 + 128 * n],
                                                          pct[0:E, 0:128 * n]),
                             wait=lt([tC], [ct_init]), inc="s")
                pct_free = evC
            COEF = lt(("s", S.counts["s"]))

        pd_ = [s_f.enter_context(nc.psum_tensor(f"pd{b}_{i}", [128, 512], F32)) for i in range(2)]
        pcb = s_f.enter_context(nc.psum_tensor(f"pcbx{b}", [128, 512], F32))

        st = dict(pgu_free=[None] * 4, pd_free=[None, None], sg_free=[None, None],
                  tm1_free=[None, None], cb_free=[None, None], pcb_free=None,
                  slotB=[[], []], slotW=[[], []], pendB=[])

        def flushB(kmax):
            n = 0
            while st["pendB"] and n < kmax:
                st["pendB"].pop(0)()
                n += 1

        for e_i in range(NEXP):
            par = e_i % 2
            is_sh = e_i >= E

            for tg in range(NTG):
                if tg == 1 and 1 <= e_i and e_i + 1 < NEXP:
                    parn = (e_i + 1) % 2
                    wfree = lt(*st["slotW"][parn]) if st["slotW"][parn] else None
                    st["slotW"][parn] = []
                    emit_wdma(e_i + 1, wait=wfree)
                hts = tg % 2
                csl = tg % 2
                flushB(10 ** 9)
                hslot_free = lt(*st["slotB"][hts]) if st["slotB"][hts] else None
                st["slotB"][hts] = []
                if not is_sh:
                    def cbm(e, e_i=e_i, tg=tg):
                        b0 = 32 * (e_i // 32)
                        return e.matmul(pcb[:, 0:TG],
                                        selb[b0:b0 + 32, 128 * (e_i % 32):128 * (e_i % 32) + 128],
                                        coefT[b0:b0 + 32, TG * tg:TG * tg + TG],
                                        start=True, stop=True,
                                        tile_position=(b0, 0))
                    tcb = S.step("tensor", cbm, wait=lt(SEL, COEF, st["pcb_free"]), inc="t")
                    ecb = S.step("scalar",
                                 lambda e, csl=csl: e.copy(coefb[csl][:], pcb[:, 0:TG]),
                                 wait=lt([tcb], st["cb_free"][csl]), inc="s")
                    st["pcb_free"] = ecb
                else:
                    ecb = None

                h_toks = []
                for ic in range(IC):
                    bp = ic % 2
                    bg, bu = 2 * bp, 2 * bp + 1
                    def gmm(e, par=par, ic=ic, tg=tg, bg=bg):
                        last = None
                        for c in range(CH):
                            last = e.matmul(pgu[bg][:, 0:TG],
                                            wg[par][:, c * I + 128 * ic: c * I + 128 * ic + 128],
                                            ybf_sl(c, TG * tg, TG),
                                            start=(c == 0), stop=(c == CH - 1))
                        return last
                    tgk = S.step("tensor", gmm,
                                 wait=lt(wload_g[par], Y_READY, st["pgu_free"][bg]), inc="t")
                    def umm(e, par=par, ic=ic, tg=tg, bu=bu):
                        last = None
                        for c in range(CH):
                            last = e.matmul(pgu[bu][:, 0:TG],
                                            wu[par][:, c * I + 128 * ic: c * I + 128 * ic + 128],
                                            ybf_sl(c, TG * tg, TG),
                                            start=(c == 0), stop=(c == CH - 1))
                        return last
                    tuk = S.step("tensor", umm, wait=lt(st["pgu_free"][bu], wload_u[par]), inc="t")
                    flushB(2)
                    ssl = ic % 2
                    stk = S.step("scalar",
                                 lambda e, bg=bg, ssl=ssl: e.activation(
                                     sg_[ssl][:], pgu[bg][:, 0:TG], AF.Silu),
                                 wait=lt([tgk], st["sg_free"][ssl]), inc="s")
                    if not is_sh:
                        v1 = S.step("vector",
                                    lambda e, ssl=ssl, bu=bu: e.tensor_tensor(
                                        tm1[ssl][:], sg_[ssl][:], pgu[bu][:, 0:TG], op=OP.mult),
                                    wait=lt([stk], [tuk], st["tm1_free"][ssl]), inc="v")
                        pp = S.step("gpsimd",
                                    lambda e, ssl=ssl, csl=csl, hts=hts, ic=ic: e.tensor_tensor(
                                        hT[:, (hts * IC + ic) * TG:(hts * IC + ic) * TG + TG],
                                        tm1[ssl][:], coefb[csl][:], op=OP.mult),
                                    wait=lt([v1], [ecb], hslot_free), inc="p")
                        st["tm1_free"][ssl] = pp
                        h_toks.append(pp)
                    else:
                        v1 = S.step("vector",
                                    lambda e, ssl=ssl, bu=bu, hts=hts, ic=ic: e.tensor_tensor(
                                        hT[:, (hts * IC + ic) * TG:(hts * IC + ic) * TG + TG],
                                        sg_[ssl][:], pgu[bu][:, 0:TG], op=OP.mult),
                                    wait=lt([stk], [tuk], hslot_free), inc="v")
                        pp = v1
                        h_toks.append(pp)
                    st["pgu_free"][bg] = stk
                    st["pgu_free"][bu] = v1
                    st["sg_free"][ssl] = v1
                if not is_sh:
                    st["cb_free"][csl] = h_toks[-1]
                HT = lt(*h_toks)

                def mk_thunk(hout, bank, par=par, tg=tg, hts=hts, HT=HT):
                    def f():
                        def dmm(e, par=par, tg=tg, hts=hts, hout=hout, bank=bank):
                            last = None
                            for ic in range(IC):
                                last = e.matmul(
                                    pd_[bank][:, 0:TG],
                                    wd[par][:, ic * C + 128 * hout: ic * C + 128 * hout + 128],
                                    hT[:, (hts * IC + ic) * TG:(hts * IC + ic) * TG + TG],
                                    start=(ic == 0), stop=(ic == IC - 1))
                            return last
                        td = S.step("tensor", dmm,
                                    wait=lt(HT, st["pd_free"][bank], wload_d[par]), inc="t")
                        ev = S.step("vector",
                                    lambda e, hout=hout, tg=tg, bank=bank: e.tensor_tensor(
                                        y32_sl(hout, TG * tg, TG), y32_sl(hout, TG * tg, TG),
                                        pd_[bank][:, 0:TG], op=OP.add),
                                    wait=[td], inc="v")
                        st["pd_free"][bank] = ev
                        st["slotB"][hts].append(ev)
                        st["slotW"][par].append(td)
                    return f
                for hout in range(CH):
                    st["pendB"].append(mk_thunk(hout, hout % 2))
        flushB(10 ** 9)
        YACC = S.barrier()
        s_f.close()

        with ExitStack() as p5:
            lnb2 = alloc_ln_bufs(nc, p5, b, 1, TG)
            pstA2 = p5.enter_context(nc.psum_tensor(f"pstA2{b}", [128, 512], F32))
            pstB2 = p5.enter_context(nc.psum_tensor(f"pstB2{b}", [128, 512], F32))
            pbc2 = p5.enter_context(nc.psum_tensor(f"pbc2{b}", [128, 1024], F32))
            pot = [p5.enter_context(nc.psum_tensor(f"pot{b}_{i}", [128, 1024], F32)) for i in range(2)]
            ostg = [p5.enter_context(nc.sbuf_tensor(f"ostg{b}_{i}", [128, C], F32)) for i in range(2)]
            ln_state = dict(prev=None)
            ln_toks = []
            for tg in range(NTG):
                ln_last = emit_ln(nc, S, cfg, tg,
                                  lambda c, tg=tg: y32[:, c * T + TG * tg: c * T + TG * tg + TG],
                                  y32_sl, ybf_sl, lnb2, pstA2, pstB2, pbc2, olnb, ob32,
                                  [YACC], ln_state)
                ln_toks.append(ln_last)
            LN2 = lt(*ln_toks)
            ostg_free = [None, None]
            pot_free = [None, None]
            for tt in range(NTT):
                sl = tt % 2
                osn = f"o{sl}"
                def otr(e, tt=tt, sl=sl):
                    last = None
                    for c in range(CH):
                        last = e.transpose(pot[sl][:, 128 * c:128 * c + 128],
                                           y32_sl(c, 128 * tt, 128), identf[:])
                    return last
                tO = S.step("tensor", otr, wait=lt(LN2, pot_free[sl]), inc="t")
                evO = S.step("vector",
                             lambda e, sl=sl: e.tensor_copy(ostg[sl][:], pot[sl][:, 0:C]),
                             wait=lt([tO], ostg_free[sl]), inc="v")
                pot_free[sl] = evO
                dO = S.dma("sync", y_out[b, 128 * tt:128 * tt + 128, :], ostg[sl][:],
                           wait=[evO], inc=osn)
                ostg_free[sl] = dO


def alloc_ln_bufs(nc, stack, b, idx, TG):
    bufs = {}
    bufs["xab"] = [stack.enter_context(nc.sbuf_tensor(f"ln{idx}xab{b}_{i}", [128, TG], BF16)) for i in range(2)]
    bufs["sqb"] = [stack.enter_context(nc.sbuf_tensor(f"ln{idx}sqb{b}_{i}", [128, TG], BF16)) for i in range(2)]
    bufs["rows"] = stack.enter_context(nc.sbuf_tensor(f"ln{idx}rows{b}", [32, 6 * TG], F32))
    bufs["rbh"] = stack.enter_context(nc.sbuf_tensor(f"ln{idx}rbh{b}", [32, 4 * TG], BF16))
    bufs["t1"] = stack.enter_context(nc.sbuf_tensor(f"ln{idx}t1{b}", [128, TG], F32))
    return bufs


def emit_ln(nc, S, cfg, tg, src_sl_fn, y_sl, ybf_sl, bufs, pstA, pstB, pbc,
            olnb, ob32, in_toks, state):
    CH, T, TG, EPS = cfg["CH"], cfg["T"], cfg["TG"], cfg["EPS"]
    xab, sqb, rows, rbh, t1 = bufs["xab"], bufs["sqb"], bufs["rows"], bufs["rbh"], bufs["t1"]
    prev = state.get("prev")
    xab_free = state.get("xab_free", [None, None])
    sqb_free = state.get("sqb_free", [None, None])
    IN = lt(*in_toks)
    stat_toks = []
    cast_toks = []
    for c in range(CH):
        sl = c % 2
        ct = S.step("vector",
                    lambda e, c=c, sl=sl: e.tensor_copy(xab[sl][:], src_sl_fn(c)),
                    wait=lt(IN, xab_free[sl], prev), inc="v")
        sq = S.step("gpsimd",
                    lambda e, sl=sl: e.tensor_tensor(sqb[sl][:], xab[sl][:], xab[sl][:], op=OP.mult),
                    wait=lt([ct], sqb_free[sl]), inc="p")
        m1 = S.step("tensor",
                    lambda e, sl=sl, c=c: e.matmul(pstA[0:32, 0:TG], olnb[:], xab[sl][:],
                                                   start=(c == 0), stop=(c == CH - 1)),
                    wait=lt([ct], state.get("pst_free") if c == 0 else None), inc="t")
        m2 = S.step("tensor",
                    lambda e, sl=sl, c=c: e.matmul(pstB[0:32, 0:TG], olnb[:], sqb[sl][:],
                                                   start=(c == 0), stop=(c == CH - 1)),
                    wait=[sq], inc="t")
        xab_free[sl] = lt(m1, sq)
        sqb_free[sl] = m2
        stat_toks.append(lt(m1, m2))
        cast_toks.append(ct)
    STAT = lt(*stat_toks)
    r1 = S.step("vector", lambda e: (
        e.tensor_copy(rows[:, 0:TG], pstA[0:32, 0:TG]),
        e.tensor_copy(rows[:, TG:2 * TG], pstB[0:32, 0:TG]))[-1],
        wait=lt(STAT, state.get("rows_free")), inc="v")
    state["pst_free"] = r1
    r2 = S.step("vector", lambda e: e.tensor_tensor(
        rows[:, 2 * TG:3 * TG], rows[:, 0:TG], rows[:, 0:TG], op=OP.mult), wait=[r1], inc="v")
    r3 = S.step("vector", lambda e: e.tensor_tensor(
        rows[:, 2 * TG:3 * TG], rows[:, TG:2 * TG], rows[:, 2 * TG:3 * TG], op=OP.subtract),
        wait=[r2], inc="v")
    r4 = S.step("vector", lambda e: e.tensor_scalar(
        rows[:, 2 * TG:3 * TG], rows[:, 2 * TG:3 * TG], float(EPS), None, op0=OP.add),
        wait=[r3], inc="v")
    r5a = S.step("scalar", lambda e: e.activation(
        rows[:, 3 * TG:4 * TG], rows[:, 2 * TG:3 * TG], AF.Ln), wait=[r4], inc="s")
    r5 = S.step("scalar", lambda e: e.activation(
        rows[:, 4 * TG:5 * TG], rows[:, 3 * TG:4 * TG], AF.Exp, scale=-0.5),
        wait=[r5a], inc="s")
    r6 = S.step("vector", lambda e: e.tensor_tensor(
        rows[:, 5 * TG:6 * TG], rows[:, 0:TG], rows[:, 4 * TG:5 * TG], op=OP.mult),
        wait=[r5], inc="v")
    r7 = S.step("vector", lambda e: (
        e.tensor_copy(rbh[:, 0:TG], rows[:, 4 * TG:5 * TG]),
        e.tensor_copy(rbh[:, 2 * TG:3 * TG], rows[:, 5 * TG:6 * TG]))[-1],
        wait=[r6], inc="v")
    r8 = S.step("vector", lambda e: (
        e.tensor_tensor(rbh[:, TG:2 * TG], rows[:, 4 * TG:5 * TG], rbh[:, 0:TG], op=OP.subtract),
        e.tensor_tensor(rbh[:, 3 * TG:4 * TG], rows[:, 5 * TG:6 * TG], rbh[:, 2 * TG:3 * TG],
                        op=OP.subtract))[-1],
        wait=[r7], inc="v")
    state["rows_free"] = r8
    bc1 = S.step("tensor", lambda e: (
        e.matmul(pbc[:, 0:TG], ob32[:], rbh[:, 0:TG], start=True, stop=False),
        e.matmul(pbc[:, 0:TG], ob32[:], rbh[:, TG:2 * TG], start=False, stop=True))[-1],
        wait=lt([r8], state.get("pbc_free")), inc="t")
    bc2 = S.step("tensor", lambda e: (
        e.matmul(pbc[:, 512:512 + TG], ob32[:], rbh[:, 2 * TG:3 * TG], start=True, stop=False),
        e.matmul(pbc[:, 512:512 + TG], ob32[:], rbh[:, 3 * TG:4 * TG], start=False, stop=True))[-1],
        wait=[bc1], inc="t")
    BC = lt(bc1, bc2)
    outs = []
    t1_free = None
    for c in range(CH):
        o1 = S.step("vector",
                    lambda e, c=c: e.tensor_tensor(t1[:], src_sl_fn(c), pbc[:, 0:TG], op=OP.mult),
                    wait=lt(BC, t1_free, cast_toks[c]), inc="v")
        o2 = S.step("vector",
                    lambda e, c=c, tg=tg: e.tensor_tensor(
                        y_sl(c, TG * tg, TG), t1[:], pbc[:, 512:512 + TG], op=OP.subtract),
                    wait=[o1], inc="v")
        o3 = S.step("vector",
                    lambda e, c=c, tg=tg: e.tensor_copy(ybf_sl(c, TG * tg, TG), y_sl(c, TG * tg, TG)),
                    wait=[o2], inc="v")
        t1_free = o2
        outs.append(o3)
    last = lt(*outs)
    state["prev"] = last
    state["pbc_free"] = last
    state["xab_free"] = xab_free
    state["sqb_free"] = sqb_free
    return last


def kernel(L0_attn_x, L0_attn_W_attn, L0_attn_W_proj, L0_attn_lambda_q1,
           L0_attn_lambda_k1, L0_attn_lambda_q2, L0_attn_lambda_k2,
           L0_ffn_router_w, L0_ffn_e_bias, L0_ffn_Wg, L0_ffn_Wu, L0_ffn_Wd,
           L0_ffn_SWg, L0_ffn_SWu, L0_ffn_SWd):
    cfg = make_cfg(full=True)
    con = host_constants(cfg)
    wattn_p = permute_wattn(cfg, L0_attn_W_attn)
    lam_rep = np.concatenate([
        np.tile(np.asarray(L0_attn_lambda_q1, dtype=np.float32)[None, :], (128, 1)),
        np.tile(np.asarray(L0_attn_lambda_k1, dtype=np.float32)[None, :], (128, 1)),
        np.tile(np.asarray(L0_attn_lambda_q2, dtype=np.float32)[None, :], (128, 1)),
        np.tile(np.asarray(L0_attn_lambda_k2, dtype=np.float32)[None, :], (128, 1))], axis=1)
    eb_rep = np.tile(np.asarray(L0_ffn_e_bias, dtype=np.float32)[None, :], (128, 1))

    nc = build_program(cfg)
    x = np.ascontiguousarray(np.asarray(L0_attn_x, dtype=np.float32))
    in_maps = []
    for core in range(N_CORES):
        m = dict(x=np.ascontiguousarray(x[2 * core:2 * core + 2]),
                 wattn=wattn_p.astype(np.float32),
                 wproj=np.ascontiguousarray(np.asarray(L0_attn_W_proj, dtype=np.float32)),
                 router_w=np.ascontiguousarray(np.asarray(L0_ffn_router_w, dtype=np.float32)),
                 Wg=np.ascontiguousarray(np.asarray(L0_ffn_Wg, dtype=np.float32)),
                 Wu=np.ascontiguousarray(np.asarray(L0_ffn_Wu, dtype=np.float32)),
                 Wd=np.ascontiguousarray(np.asarray(L0_ffn_Wd, dtype=np.float32)),
                 SWg=np.ascontiguousarray(np.asarray(L0_ffn_SWg, dtype=np.float32)),
                 SWu=np.ascontiguousarray(np.asarray(L0_ffn_SWu, dtype=np.float32)),
                 SWd=np.ascontiguousarray(np.asarray(L0_ffn_SWd, dtype=np.float32)),
                 lam_rep=lam_rep, ebias_rep=eb_rep)
        for k, v in con.items():
            m["c_" + k] = v.astype(np.float32)
        in_maps.append(m)
    res = run_bass_kernel_spmd(nc, in_maps, core_ids=list(range(N_CORES)))
    out = np.concatenate([np.asarray(r["y"]) for r in res.results], axis=0)
    return out.astype(np.float32)
